# version 54
# baseline (speedup 1.0000x reference)
import numpy as np
import ml_dtypes
from contextlib import ExitStack
import concourse.bass as bass
import concourse.mybir as mybir
from concourse.alu_op_type import AluOpType as ALU
from concourse.bass_utils import run_bass_kernel_spmd

bf16 = ml_dtypes.bfloat16
F32 = mybir.dt.float32
BF16 = mybir.dt.bfloat16
AF = mybir.ActivationFunctionType
AX = mybir.AxisListType

T = 4096
QT = 512
D = 1024
NCH = 8
NP = 8
EXP_ORDER = [16, 17] + list(range(16))
STAGE = 99
LIMIT_PAIRS = None
LIMIT_QT = None
SKIP_K = False
SKIP_V = False
SKIP_KNORM = False
SKIP_KEVAC = False

ENGS = ("tensor", "scalar", "vector", "gpsimd", "sync")
OWN = {"tensor": "pe", "scalar": "act", "vector": "dve"}
SEMS = ("pe", "act", "dve", "qw", "qx", "qs", "pl")


def _flat(deps):
    for d in deps:
        if d is None:
            continue
        if isinstance(d, list):
            yield from _flat(d)
        else:
            yield d


class Prog:

    def __init__(self):
        self.st = {e: [] for e in ENGS}
        self.cnt = {s: 0 for s in SEMS}
        self.lastw = {e: {} for e in ENGS}

    def emit(self, eng, fn, deps=(), inc=None, nochain=False):
        own = OWN.get(eng)
        if eng in ("vector", "scalar"):
            if not nochain:
                deps = list(deps) + [(own, self.cnt[own])]
            if inc is None and fn is not None:
                inc = own
        n = 16 if inc in ("qw", "qx", "qs") else 1
        if n == 16:
            deps = list(deps) + [(inc, self.cnt[inc])]
        waits = []
        seen = {}
        for s, v in _flat(deps):
            if s == "pe" and eng == "tensor" and v <= self.cnt[s]:
                continue
            if s in ("qw", "qx", "qs"):
                if v > seen.get(s, 0):
                    seen[s] = v
            elif v > self.lastw[eng].get(s, 0):
                seen[s] = max(seen.get(s, 0), v)
                self.lastw[eng][s] = v
        waits = sorted(seen.items())
        tok = None
        if inc is not None:
            self.cnt[inc] += n
            tok = (inc, self.cnt[inc])
        self.st[eng].append((waits, fn, inc, n))
        return tok

    def wait_only(self, eng, deps):
        self.emit(eng, None, deps=deps)


def build_program():
    nc = bass.Bass("TRN2", target_bir_lowering=False)
    TH = T // 2
    NQ = TH // QT
    NJ = T // 128
    NKT = T // QT
    E2 = 2 * QT
    NST = QT // 128
    AHI = NP * TH
    CWB = AHI + 4 * QT
    ABW = AHI + max(NCH * QT, 4 * QT + TH)
    ACC = NCH * TH

    xti = nc.dram_tensor("xti", [128, NCH * T], F32, kind="ExternalInput").ap()
    attnw = nc.dram_tensor("attnw", [NP, 128, 3072], F32, kind="ExternalInput").ap()
    woi = nc.dram_tensor("woi", [128, 8192], F32, kind="ExternalInput").ap()
    boi = nc.dram_tensor("boi", [1, 1024], F32, kind="ExternalInput").ap()
    wri = nc.dram_tensor("wri", [128, 128], F32, kind="ExternalInput").ap()
    moew = nc.dram_tensor("moew", [18, 128, 12288], F32, kind="ExternalInput").ap()
    cid = nc.dram_tensor("cid", [128, 128], BF16, kind="ExternalInput").ap()
    cidf = nc.dram_tensor("cidf", [128, 128], F32, kind="ExternalInput").ap()
    ceim = nc.dram_tensor("ceim", [16, 2048], BF16, kind="ExternalInput").ap()
    scr = nc.dram_tensor("scr", [128, NCH * TH], BF16, kind="Internal").ap()
    scr2 = nc.dram_tensor("scr2", [NP, 128, T + 130 * (T // 128)], BF16, kind="Internal").ap()
    out_d = nc.dram_tensor("out", [128, NCH * T], F32, kind="ExternalOutput").ap()

    P = Prog()
    ctx = ExitStack()
    sbt = lambda name, shape, dt: ctx.enter_context(nc.sbuf_tensor(name, shape, dt))
    pst = lambda name, shape, dt: ctx.enter_context(nc.psum_tensor(name, shape, dt))

    with ctx:
        xt = sbt("xt", [128, NCH * T], BF16)
        abuf = sbt("abuf", [128, ABW], BF16)
        wb = [sbt("wbA", [128, 12288], BF16), sbt("wbB", [128, 12288], BF16)]
        qtb = sbt("qtb", [128, 4 * QT], BF16)
        pt = sbt("pt", [128, 3 * E2], BF16)
        sq = sbt("sq", [128, QT], BF16)
        sqk = sbt("sqk", [128, 2 * QT], BF16)
        ab2 = sbt("ab2", [128, 2 * QT], BF16)
        cwT = sbt("cwT", [16, T], BF16)
        rowsA = sbt("rowsA", [128, 4 * QT + 24], F32)
        rbf = sbt("rbf", [1, 2 * QT], BF16)
        nbf = sbt("nbf", [1, 8], BF16)
        bob = sbt("bob", [1, 1024], BF16)
        biasb = sbt("biasb", [128, 12], F32)
        recipb = sbt("recipb", [64, 2 * QT], F32)
        astg = sbt("astg", [128, 2 * QT], F32)
        prs = astg[:, QT:QT + NST * 16]
        cwf = astg[:, QT + NST * 16:QT + 2 * NST * 16]
        outst = sbt("outst", [128, 2 * QT], F32)
        wrb = sbt("wrb", [128, 128], BF16)
        xf32 = sbt("xf32", [128, QT], F32)
        cidb = sbt("cidb", [128, 128], BF16)
        cidfb = sbt("cidfb", [128, 128], F32)
        ceimb = sbt("ceimb", [16, 2048], BF16)
        ones_c = sbt("ones_c", [128, 1], BF16)
        ones_r = sbt("ones_r", [1, QT], BF16)
        ones_f = sbt("ones_f", [128, 64], F32)

        psS = [pst("psS0", [128, E2], F32), pst("psS1", [128, E2], F32)]
        psA = [pst("psA0", [128, QT], F32), pst("psA1", [128, QT], F32)]
        psM = pst("psM", [128, QT], F32)
        psV = pst("psV", [128, QT], F32)

        kt_v = wb[1][:, 8192:8192 + T]
        va_v = wb[0][:, 6144:6144 + NJ * 130]

        R_m = rowsA[0:1, 0:QT]
        R_v = rowsA[0:1, QT:2 * QT]
        R_t = rowsA[0:1, 2 * QT:3 * QT]
        R_a = rowsA[0:1, 3 * QT:4 * QT]
        SC_kn = rowsA[0:1, 4 * QT:4 * QT + 1]
        SC_q = rowsA[0:1, 4 * QT + 1:4 * QT + 2]
        SC_c = rowsA[0:1, 4 * QT + 2:4 * QT + 3]
        SC_t = rowsA[0:1, 4 * QT + 3:4 * QT + 4]
        SC_eps = rowsA[0:1, 4 * QT + 4:4 * QT + 5]
        KSLOT = lambda p: rowsA[0:1, 4 * QT + 5 + p:4 * QT + 6 + p]
        SC_q4 = [rowsA[0:1, 4 * QT + 13 + i:4 * QT + 14 + i] for i in range(4)]
        SC_t4 = [rowsA[0:1, 4 * QT + 17 + i:4 * QT + 18 + i] for i in range(4)]

        tk_cid = P.emit("sync", lambda e: e.dma_start(cidb[:, :], cid), inc="qx")
        tk_cidf = P.emit("sync", lambda e: e.dma_start(cidfb[:, :], cidf), inc="qx")
        tk_ceim = P.emit("sync", lambda e: e.dma_start(ceimb[:, :], ceim), inc="qx")
        tk_wr = P.emit("gpsimd", lambda e: e.dma_start(wrb[:, :], wri), inc="qw")
        qkv_tok0 = P.emit("gpsimd",
                          lambda e: e.dma_start(wb[0][:, 0:3072], attnw[0, :, :]),
                          inc="qw")
        qkv_tok1 = P.emit("gpsimd",
                          lambda e: e.dma_start(wb[0][:, 3072:6144], attnw[1, :, :]),
                          inc="qw")
        xt3 = xt[:, :].rearrange("p (c t) -> p c t", c=NCH)
        xti3 = xti.rearrange("p (c t) -> p c t", c=NCH)
        P.emit("vector", lambda e: e.memset(SC_eps, 1e-5))
        P.emit("vector", lambda e: e.memset(ones_c[:, :], 1.0))
        P.emit("vector", lambda e: e.memset(ones_r[:, :], 1.0))
        P.emit("vector", lambda e: e.memset(ones_f[:, :], 1.0))
        va3 = va_v.rearrange("p (a b) -> p a b", b=130)
        P.emit("vector", lambda e, v=va3: e.memset(v[:, :, 64:65], 1.0))
        tk_ones = P.emit("vector", lambda e, v=va3: e.memset(v[:, :, 129:130], 1.0),
                         inc="dve")
        tk_xts = []
        for ki in range(T // QT):
            tk_xts.append(P.emit(
                "gpsimd",
                lambda e, ki=ki: e.dma_start(xt3[:, :, ki * QT:(ki + 1) * QT],
                                             xti3[:, :, ki * QT:(ki + 1) * QT]),
                inc="qw"))
        tk_xt = tk_xts[-1]
        tk_bo = P.emit("gpsimd", lambda e: e.dma_start(bob[:, :], boi), inc="qw")

        qkv_tok = [None, None]
        qkv_war = [[tk_ones], [tk_ones]]

        def load_qkv(p):
            s = p % 2
            qkv_tok[s] = P.emit(
                "gpsimd",
                lambda e, p=p, s=s: e.dma_start(
                    wb[0][:, 3072 * s:3072 * s + 3072], attnw[p, :, :]),
                deps=[qkv_war[s]], inc="qw")

        qkv_tok[0] = qkv_tok0
        qkv_tok[1] = qkv_tok1

        def qkvw(s, off, kc):
            return wb[0][:, 3072 * s + off + kc * 128:3072 * s + off + (kc + 1) * 128]

        war = {
            "psS0a": [tk_ones], "psS0b": [tk_ones],
            "psS1a": [tk_ones], "psS1b": [tk_ones],
            "psA0": [], "psA1": [], "psM": [], "psT": [],
            "kt": [], "va": [tk_ones],
            "qtw": [[], [], [], []], "biasw": [[], [], [], []], "astg": [],
            "sq": [], "sqk0": [], "sqk1": [], "psV": [],
            "xf": [], "H0": [], "H1": [],
        }
        st_outst = [None, None]
        moe_slot_war = [[], []]
        kvbank = [(psM, "psM")]

        def attention_half(h, abuf_free_tok, next_qkv_slot=None,
                           pend_start=None):
            s_last = [None]
            pv_last = [None]
            a_last = None
            final_aw = [None] * NQ
            kv_state = {}
            next_chains = {}
            pend_tail = [None]
            pend_cp = [None]
            cps_box = [None]

            def k_group(s, ki, kst):
                pb, pw = kvbank[0]
                kvbank[0] = (psV, "psV") if pw == "psM" else (psM, "psM")
                mm = None
                for kc in range(NCH):
                    dep = [qkv_tok[s], war[pw], tk_xts[ki]] if kc == 0 else []
                    mm = P.emit(
                        "tensor",
                        lambda e, s=s, kc=kc, ki=ki, pb=pb:
                            e.matmul(pb[:, 0:QT],
                                     qkvw(s, 1024, kc),
                                     xt[:, kc * T + ki * QT:kc * T + (ki + 1) * QT],
                                     start=(kc == 0), stop=(kc == NCH - 1)),
                        deps=dep, inc="pe" if kc == NCH - 1 else None)
                ev = P.emit("vector",
                            lambda e, ki=ki, pb=pb:
                                e.tensor_copy(kt_v[:, ki * QT:(ki + 1) * QT],
                                              pb[:, 0:QT]),
                            deps=[mm, war["kt"]], inc="dve", nochain=True)
                kst["kt_ready"] = ev
                war[pw] = [ev]
                sqt = P.emit("vector",
                             lambda e, ki=ki:
                                 e.tensor_tensor(sqk[:, (ki % 2) * QT:(ki % 2 + 1) * QT],
                                                 kt_v[:, ki * QT:(ki + 1) * QT],
                                                 kt_v[:, ki * QT:(ki + 1) * QT],
                                                 ALU.mult),
                             deps=[ev, war["sqk%d" % (ki % 2)]], inc="dve",
                             nochain=True)
                kst.setdefault("pend", []).append((ki, sqt))

            def knorm_one(kst):
                ki, sqt = kst["pend"].pop(0)
                knm = P.emit("tensor",
                             lambda e, ki=ki:
                                 e.matmul(psM[0:1, 0:QT], ones_c[:, :],
                                          sqk[:, (ki % 2) * QT:(ki % 2 + 1) * QT],
                                          start=True, stop=True),
                             deps=[sqt, war["psM"]], inc="pe")
                war["sqk%d" % (ki % 2)] = [knm]
                rm = P.emit("vector",
                            lambda e: e.tensor_reduce(SC_t, psM[0:1, 0:QT],
                                                      AX.X, ALU.max),
                            deps=[knm], inc="dve")
                war["psM"] = [rm]
                ks = KSLOT(kst["pidx"])
                if ki == 0:
                    kst["kmax"] = P.emit("vector",
                                         lambda e, ks=ks: e.tensor_copy(ks, SC_t),
                                         deps=[rm], inc="dve")
                else:
                    kst["kmax"] = P.emit("vector",
                                         lambda e, ks=ks:
                                             e.tensor_tensor(ks, ks, SC_t, ALU.max),
                                         deps=[rm], inc="dve")

            def knorm_batch(kst):
                while kst.get("pend"):
                    knorm_one(kst)
                kst.pop("pend", None)

            def v_group(s, ki, kst):
                pv_, pw = kvbank[0]
                kvbank[0] = (psV, "psV") if pw == "psM" else (psM, "psM")
                mm = None
                for jt4 in range(QT // 128):
                    jt = ki * (QT // 128) + jt4
                    for kc in range(NCH):
                        dep = [qkv_tok[s], war[pw], tk_xts[ki]] \
                            if (jt4 == 0 and kc == 0) else []
                        mm = P.emit(
                            "tensor",
                            lambda e, s=s, kc=kc, jt=jt, jt4=jt4, pv_=pv_:
                                e.matmul(pv_[:, jt4 * 128:(jt4 + 1) * 128],
                                         xt[:, kc * T + jt * 128:kc * T + (jt + 1) * 128],
                                         qkvw(s, 2048, kc),
                                         start=(kc == 0), stop=(kc == NCH - 1)),
                            deps=dep, inc="pe" if kc == NCH - 1 else None)
                c0 = P.emit(
                    "vector",
                    lambda e, ki=ki, pv_=pv_:
                        e.tensor_copy(
                            va3[:, ki * 4:(ki + 1) * 4, 0:64],
                            pv_[:, :].rearrange("p (a b) -> p a b", b=128)[:, :, 0:64]),
                    deps=[mm, war["va"]], inc="dve", nochain=True)
                c1 = P.emit(
                    "vector",
                    lambda e, ki=ki, pv_=pv_:
                        e.tensor_copy(
                            va3[:, ki * 4:(ki + 1) * 4, 65:129],
                            pv_[:, :].rearrange("p (a b) -> p a b", b=128)[:, :, 64:128]),
                    deps=[mm], inc="dve", nochain=True)
                war[pw] = [c0, c1]
                kst["va_ready"] = [c0, c1]

            def kv_full(s, pidx):
                kst = {"pidx": pidx}
                for ki in range(NKT):
                    k_group(s, ki, kst)
                    v_group(s, ki, kst)
                    knorm_one(kst)
                knorm_batch(kst)
                return kst

            def kv_load_slice(pnext, ki, kst, dep):
                lk = P.emit("sync",
                            lambda e, pnext=pnext, ki=ki:
                                e.dma_start(kt_v[:, ki * QT:(ki + 1) * QT],
                                            scr2[pnext, :, ki * QT:(ki + 1) * QT]),
                            deps=[dep], inc="qx")
                nv = (QT // 128) * 130
                lv = P.emit("sync",
                            lambda e, pnext=pnext, ki=ki, nv=nv:
                                e.dma_start(va_v[:, ki * nv:(ki + 1) * nv],
                                            scr2[pnext, :, T + ki * nv:
                                                 T + (ki + 1) * nv]),
                            deps=[dep], inc="qx")
                kst["kt_ready"] = lk
                kst["va_ready"] = lv

            serial_kv = LIMIT_PAIRS is not None or LIMIT_QT is not None
            npair = NP if LIMIT_PAIRS is None else LIMIT_PAIRS
            for p in range(npair):
                s = p % 2
                if p + 1 < NP:
                    if not (h == 0 and p == 0):
                        load_qkv(p + 1)
                elif h == 0:
                    load_qkv(0)
                if p == 0 and h == 1 and war.get("kv_next") is not None:
                    kst = war.pop("kv_next")
                elif p in kv_state:
                    kst = kv_state.pop(p)
                else:
                    kst = kv_full(s, p)
                kmax_tok = kst["kmax"]
                kt_ready = kst["kt_ready"]
                va_ready = kst["va_ready"]
                if h == 0 and not serial_kv:
                    spk = P.emit("sync",
                                 lambda e, p=p: e.dma_start(scr2[p, :, 0:T],
                                                            kt_v[:, :]),
                                 deps=[kt_ready, s_last[0]], inc="qx")
                    spv = P.emit("sync",
                                 lambda e, p=p:
                                     e.dma_start(scr2[p, :, T:T + NJ * 130],
                                                 va_v[:, :]),
                                 deps=[va_ready, pv_last[0]], inc="qx")
                    war["kt"] = war["kt"] + [spk]
                    war["va"] = war["va"] + [spv]

                def next_slot():
                    if serial_kv:
                        return None
                    if p + 1 < npair:
                        return (p + 1, (p + 1) % 2)
                    if h == 0 and next_qkv_slot is not None:
                        return ("next_half", 0)
                    return None

                qtw = war["qtw"]
                biasw = war["biasw"]

                def q_chain_A(q, sx_=None):
                    ss = s if sx_ is None else sx_
                    sl = q % 4
                    pb, pw = (psM, "psM") if q % 2 == 0 else (psV, "psV")
                    tq0 = h * TH + q * QT
                    qmm = None
                    for kc in range(NCH):
                        dep = [war[pw], qkv_tok[ss],
                               tk_xts[(h * TH + q * QT) // QT]] if kc == 0 else []
                        qmm = P.emit(
                            "tensor",
                            lambda e, ss=ss, kc=kc, tq0=tq0, pb=pb:
                                e.matmul(pb[:, 0:QT],
                                         qkvw(ss, 0, kc),
                                         xt[:, kc * T + tq0:kc * T + tq0 + QT],
                                         start=(kc == 0), stop=(kc == NCH - 1)),
                            deps=dep, inc="pe" if kc == NCH - 1 else None)
                    qkv_war[ss] = [qmm]
                    qe = P.emit("vector",
                                lambda e, sl=sl, pb=pb:
                                    e.tensor_scalar(qtb[:, sl * QT:(sl + 1) * QT],
                                                    pb[:, 0:QT], 0.125, None,
                                                    ALU.mult),
                                deps=[qmm, qtw[sl]], inc="dve", nochain=True)
                    war[pw] = [qe]
                    return qe

                def q_chain_B(q, qe):
                    sl = q % 4
                    pb, pw = (psM, "psM") if q % 2 == 0 else (psV, "psV")
                    qs = P.emit("vector",
                                lambda e, sl=sl, q=q:
                                    e.tensor_tensor(sqk[:, (q % 2) * QT:(q % 2 + 1) * QT],
                                                    qtb[:, sl * QT:(sl + 1) * QT],
                                                    qtb[:, sl * QT:(sl + 1) * QT],
                                                    ALU.mult),
                                deps=[qe, war["sqk%d" % (q % 2)]], inc="dve",
                                nochain=True)
                    qn = P.emit("tensor",
                                lambda e, q=q: e.matmul(pb[0:1, 0:QT], ones_c[:, :],
                                                        sqk[:, (q % 2) * QT:
                                                            (q % 2 + 1) * QT],
                                                        start=True, stop=True)
                                if False else
                                    e.matmul(pb[0:1, 0:QT], ones_c[:, :],
                                             sqk[:, (q % 2) * QT:(q % 2 + 1) * QT],
                                             start=True, stop=True),
                                deps=[qs, war[pw]], inc="pe")
                    war["sqk%d" % (q % 2)] = [qn]
                    qm = P.emit("vector",
                                lambda e, pb=pb: e.tensor_reduce(
                                    SC_q4[q], pb[0:1, 0:QT], AX.X, ALU.max),
                                deps=[qn], inc="dve")
                    war[pw] = [qm]
                    return qm

                def q_chain_C_all(qms, nq):
                    sc4 = rowsA[0:1, 4 * QT + 13:4 * QT + 13 + nq]
                    st4 = rowsA[0:1, 4 * QT + 17:4 * QT + 17 + nq]
                    s3 = sc4.rearrange("p (a b) -> p a b", b=1)
                    k3 = KSLOT(p).rearrange("p (a b) -> p a b", b=1) \
                        .broadcast_to([1, nq, 1])
                    cc = P.emit("vector",
                                lambda e, s3=s3, k3=k3:
                                    e.tensor_tensor(s3, s3, k3, ALU.mult),
                                deps=[qms, kmax_tok], inc="dve")
                    l1 = P.emit("scalar",
                                lambda e, sc4=sc4, st4=st4:
                                    e.activation(st4, sc4, AF.Ln),
                                deps=[cc], inc="act")
                    e1 = P.emit("scalar",
                                lambda e, sc4=sc4, st4=st4:
                                    e.activation(sc4, st4, AF.Exp, scale=0.5),
                                deps=[l1], inc="act")
                    ng = P.emit("vector",
                                lambda e, sc4=sc4, nq=nq:
                                    e.tensor_scalar(nbf[0:1, 0:nq], sc4,
                                                    -1.0, None, ALU.mult),
                                deps=[e1], inc="dve")
                    bc = P.emit("tensor",
                                lambda e, nq=nq:
                                    e.matmul(psM[:, 0:nq], ones_r[0:1, 0:128],
                                             nbf[0:1, 0:nq],
                                             start=True, stop=True),
                                deps=[ng, war["psM"]], inc="pe")
                    be = P.emit("vector",
                                lambda e, nq=nq:
                                    e.tensor_copy(biasb[:, 0:nq], psM[:, 0:nq]),
                                deps=[bc] + [biasw[i] for i in range(nq)],
                                inc="dve", nochain=True)
                    war["psM"] = [be]
                    return be

                nq = NQ if LIMIT_QT is None else LIMIT_QT
                pre = next_chains.pop(p, {})
                _qe = {}
                for qq in range(nq):
                    _qe[qq] = pre[("qe", qq)] if ("qe", qq) in pre \
                        else q_chain_A(qq)
                qms = [pre[("qm", qq)] if ("qm", qq) in pre
                       else q_chain_B(qq, _qe[qq]) for qq in range(nq)]
                if pend_cp[0] is not None:
                    cps_box[0] = pend_cp[0]()
                    pend_cp[0] = None
                if p == 0 and pend_start is not None:
                    pend_start()
                    pend_start = None
                be_box = [None]
                for q in range(nq):
                    sl_q = q % 4
                    qe = _qe[q]
                    last_q = (q == nq - 1)
                    ns = next_slot() if last_q else None
                    nkst = {}
                    if ns is not None and h == 0:
                        nkst["pidx"] = ns[0] if ns[0] != "next_half" else 0
                    s_tok = [None] * NJ
                    e_tok = [None] * NJ
                    pv_tok = [None] * NJ

                    def emit_S(j, extra=(), kt_ready=kt_ready, qe=qe, sl_q=sl_q):
                        sl = j % 2
                        deps0 = [kt_ready, qe] + list(extra)
                        if j < 2:
                            deps0 += [war["psS0a"], war["psS0b"]] if sl == 0 else \
                                     [war["psS1a"], war["psS1b"]]
                        P.emit("tensor",
                               lambda e, j=j, sl=sl, sl_q=sl_q:
                                   e.matmul(psS[sl][:, 0:QT],
                                            kt_v[0:64, j * 128:(j + 1) * 128],
                                            qtb[0:64, sl_q * QT:(sl_q + 1) * QT],
                                            start=True, stop=True),
                               deps=deps0)
                        s_tok[j] = P.emit(
                            "tensor",
                            lambda e, j=j, sl=sl, sl_q=sl_q:
                                e.matmul(psS[sl][:, QT:E2],
                                         kt_v[64:128, j * 128:(j + 1) * 128],
                                         qtb[64:128, sl_q * QT:(sl_q + 1) * QT],
                                         start=True, stop=True),
                            inc="pe")
                        s_last[0] = s_tok[j]

                    def emit_E(j, sl_q=sl_q):
                        sl = j % 3
                        deps0 = [s_tok[j], be_box[0]]
                        if j >= 3:
                            deps0.append(pv_tok[j - 3])
                        e_tok[j] = P.emit(
                            "scalar",
                            lambda e, j=j, sl=sl, sl_q=sl_q:
                                e.activation(pt[:, sl * E2:(sl + 1) * E2],
                                             psS[j % 2][:, 0:E2], AF.Exp,
                                             bias=biasb[:, sl_q:sl_q + 1]),
                            deps=deps0, inc="act", nochain=True)

                    def emit_PV(j, va_ready=va_ready):
                        sl = j % 3
                        deps0 = [e_tok[j], va_ready]
                        if j == 0:
                            deps0 += [war["psA0"], war["psA1"]]
                        P.emit("tensor",
                               lambda e, j=j, sl=sl:
                                   e.matmul(psA[0][0:65, :],
                                            va_v[:, j * 130:j * 130 + 65],
                                            pt[:, sl * E2:sl * E2 + QT],
                                            start=(j == 0), stop=(j == NJ - 1)),
                               deps=deps0)
                        pv_tok[j] = P.emit(
                            "tensor",
                            lambda e, j=j, sl=sl:
                                e.matmul(psA[1][0:65, :],
                                         va_v[:, j * 130 + 65:j * 130 + 130],
                                         pt[:, sl * E2 + QT:sl * E2 + E2],
                                         start=(j == 0), stop=(j == NJ - 1)),
                            inc="pe")
                        pv_last[0] = pv_tok[j]

                    emit_S(0)
                    emit_S(1)
                    if q == 0:
                        be_box[0] = q_chain_C_all(qms, nq)
                    for j in range(NJ):
                        emit_E(j)
                        if j + 2 < NJ:
                            emit_S(j + 2, extra=[e_tok[j]])
                        emit_PV(j)
                        if j == 2 and pend_tail[0] is not None:
                            pend_tail[0]()
                            pend_tail[0] = None
                        if (h == 1 and last_q and not serial_kv
                                and p + 1 < npair and j in (5, 6, 9, 10, 13, 14)):
                            qi, isB = {5: (0, False), 6: (0, True),
                                       9: (1, False), 10: (1, True),
                                       13: (2, False), 14: (2, True)}[j]
                            if qi < nq - 1:
                                nc_ = next_chains.setdefault(p + 1, {})
                                if isB:
                                    nc_[("qm", qi)] = q_chain_B(
                                        qi, nc_[("qe", qi)])
                                else:
                                    nc_[("qe", qi)] = q_chain_A(
                                        qi, sx_=(p + 1) % 2)
                        if ns is not None and j % 4 == 3:
                            ki = j // 4
                            if h == 0:
                                k_group(ns[1], ki, nkst)
                                v_group(ns[1], ki, nkst)
                                knorm_one(nkst)
                            else:
                                kv_load_slice(p + 1, ki, nkst,
                                              [s_tok[j], pv_tok[j]])
                    if ns is not None:
                        if h == 0:
                            knorm_batch(nkst)
                            nkst["pidx"] = ns[0] if ns[0] != "next_half" else 0
                        else:
                            nkst["kmax"] = war["kmax_toks"][p + 1]
                            nkst["pidx"] = p + 1
                        kv_state[p + 1 if ns[0] != "next_half" else "next_half"] = nkst
                    qtw[sl_q] = [s_tok[NJ - 1]]
                    biasw[sl_q] = [e_tok[NJ - 1]]
                    war["psS0a"] = war["psS0b"] = [e_tok[NJ - 2]]
                    war["psS1a"] = war["psS1b"] = [e_tok[NJ - 1]]
                    def emit_cps(pvlast=pv_tok[NJ - 1]):
                        c0 = P.emit("vector",
                                    lambda e: e.tensor_copy(astg[0:65, 0:QT],
                                                            psA[0][0:65, :]),
                                    deps=[pvlast, war["astg"]],
                                    inc="dve", nochain=True)
                        c1 = P.emit("vector",
                                    lambda e: e.tensor_copy(astg[0:65, QT:E2],
                                                            psA[1][0:65, :]),
                                    deps=[pvlast], inc="dve", nochain=True)
                        war["psA0"] = [c0]
                        war["psA1"] = [c1]
                        return c0, c1

                    if last_q and not serial_kv and p + 1 < npair:
                        pend_cp[0] = emit_cps
                        cp0 = cp1 = None
                    else:
                        cp0, cp1 = emit_cps()

                    def tail(p=p, q=q, cp0=cp0, cp1=cp1):
                        if cp0 is None:
                            cp0, cp1 = cps_box[0]
                        bm0 = P.emit("tensor",
                                     lambda e: e.matmul(psM[0:64, 0:QT],
                                                        ones_f[64:65, :],
                                                        astg[64:65, 0:QT],
                                                        start=True, stop=True),
                                     deps=[cp0, war["psM"]], inc="pe")
                        rb0 = P.emit("vector",
                                     lambda e: e.tensor_copy(recipb[:, 0:QT],
                                                             psM[0:64, 0:QT]),
                                     deps=[bm0], inc="dve", nochain=True)
                        bm1 = P.emit("tensor",
                                     lambda e: e.matmul(psM[0:64, 0:QT],
                                                        ones_f[64:65, :],
                                                        astg[64:65, QT:E2],
                                                        start=True, stop=True),
                                     deps=[cp1, rb0], inc="pe")
                        rb1 = P.emit("vector",
                                     lambda e: e.tensor_copy(recipb[:, QT:E2],
                                                             psM[0:64, 0:QT]),
                                     deps=[bm1], inc="dve", nochain=True)
                        war["psM"] = [rb1]
                        rcp = P.emit("vector",
                                     lambda e: e.reciprocal(recipb[:, 0:E2],
                                                            recipb[:, 0:E2]),
                                     deps=[rb1], inc="dve", nochain=True)
                        aw0 = P.emit("vector",
                                     lambda e, p=p, q=q:
                                         e.tensor_tensor(
                                             abuf[0:64, p * TH + q * QT:p * TH + (q + 1) * QT],
                                             astg[0:64, 0:QT], recipb[0:64, 0:QT],
                                             ALU.mult),
                                     deps=[rcp, abuf_free_tok], inc="dve", nochain=True)
                        aw1 = P.emit("vector",
                                     lambda e, p=p, q=q:
                                         e.tensor_tensor(
                                             abuf[64:128, p * TH + q * QT:p * TH + (q + 1) * QT],
                                             astg[0:64, QT:E2], recipb[0:64, QT:E2],
                                             ALU.mult),
                                     deps=[rcp, abuf_free_tok], inc="dve", nochain=True)
                        war["astg"] = [aw0, aw1]
                        if p == npair - 1:
                            final_aw[q] = [aw0, aw1]
                        return aw1

                    pend_tail[0] = tail
                war["kt"] = [s_last[0]] + [t for t in war["kt"]
                                           if isinstance(t, tuple) and t[0] == "qx"]
                war["va"] = [pv_last[0]] + [t for t in war["va"]
                                           if isinstance(t, tuple) and t[0] == "qx"]
                if h == 0:
                    war.setdefault("kmax_toks", {})[p] = kmax_tok
            if pend_tail[0] is not None:
                a_last = pend_tail[0]()
                pend_tail[0] = None
            war["kv_next"] = kv_state.pop("next_half", None)
            return final_aw, s_last[0], pv_last[0]

        def wo_ln1_half(h, a_done):
            xn1_tok = [[None] * NCH for _ in range(NQ)]
            last_xt_read = [None]
            wo_last = [None]
            ln_done = {-1: []}
            stats_tok = {}
            BANKS = [(psS[0], 0, "psS0a"), (psS[1], 0, "psS1a"),
                     (psS[0], QT, "psS0b"), (psS[1], QT, "psS1b")]

            def wo_mms(q, mc):
                pss, off, pb = BANKS[mc % 4]
                for kc in range(NCH):
                    dep = [a_done[q], war[pb], tk_wo] if kc == 0 else []
                    P.emit(
                        "tensor",
                        lambda e, kc=kc, mc=mc, q=q, pss=pss, off=off:
                            e.matmul(pss[:, off:off + QT],
                                     wb[1][:, kc * 1024 + mc * 128:
                                           kc * 1024 + (mc + 1) * 128],
                                     abuf[:, kc * TH + q * QT:kc * TH + (q + 1) * QT],
                                     start=(kc == 0), stop=False),
                        deps=dep)
                mm = P.emit("tensor",
                            lambda e, mc=mc, pss=pss, off=off:
                                e.matmul(pss[:, off:off + QT],
                                         bob[0:1, mc * 128:(mc + 1) * 128],
                                         ones_r[0:1, 0:QT], start=False, stop=True),
                            deps=[tk_bo], inc="pe")
                wo_last[0] = mm
                return mm

            def wo_rt(q, mc, mm):
                pss, off, pb = BANKS[mc % 4]
                rt = P.emit(
                    "vector",
                    lambda e, mc=mc, q=q, pss=pss, off=off, h=h:
                        e.tensor_tensor(
                            abuf[:, AHI + mc * QT:AHI + (mc + 1) * QT],
                            pss[:, off:off + QT],
                            xt[:, mc * T + h * TH + q * QT:
                               mc * T + h * TH + (q + 1) * QT],
                            ALU.add),
                    deps=[mm] + ln_done[q - 1], inc="dve")
                last_xt_read[0] = rt
                war[pb] = [rt]
                return rt

            def stats(q, r_tok):
                sx = sx2 = sq_last = None
                for mc in range(NCH):
                    rsl = abuf[:, AHI + mc * QT:AHI + (mc + 1) * QT]
                    dep0 = [r_tok[mc]] + ([war["psA0"]] if mc == 0 else [])
                    sx = P.emit("tensor",
                                lambda e, rsl=rsl, mc=mc:
                                    e.matmul(psA[0][0:1, 0:QT], ones_c[:, :],
                                             rsl, start=(mc == 0),
                                             stop=(mc == NCH - 1)),
                                deps=dep0, inc="pe" if mc == NCH - 1 else None)
                    sqt = P.emit("scalar",
                                 lambda e, rsl=rsl, mc=mc:
                                     e.activation(sqk[:, (mc % 2) * QT:(mc % 2 + 1) * QT],
                                                  rsl, AF.Square),
                                 deps=[r_tok[mc], war["sqk%d" % (mc % 2)]],
                                 inc="act", nochain=True)
                    sq_last = sqt
                    sx2 = P.emit("tensor",
                                 lambda e, mc=mc:
                                     e.matmul(psA[1][0:1, 0:QT], ones_c[:, :],
                                              sqk[:, (mc % 2) * QT:(mc % 2 + 1) * QT],
                                              start=(mc == 0),
                                              stop=(mc == NCH - 1)),
                                 deps=[sqt] + ([war["psA1"]] if mc == 0 else []),
                                 inc="pe")
                    war["sqk%d" % (mc % 2)] = [sx2]
                ab_, bb_ = ln_rows_pre(sx, sx2, psA[0], psA[1])
                war["psA0"] = war["psA1"] = [ab_]
                stats_tok[q] = (ab_, bb_, sx, sq_last)

            def rows_norm(q, wo_mm_last):
                ab_, bb_, sx, sq_last = stats_tok.pop(q)
                be2, ae = ln_rows_post(ab_, bb_)
                n_last = None
                for mc in range(NCH):
                    rsl = abuf[:, AHI + mc * QT:AHI + (mc + 1) * QT]
                    P.emit("vector",
                           lambda e, mc=mc, q=q, rsl=rsl:
                               e.tensor_tensor(abuf[:, mc * TH + q * QT:mc * TH + (q + 1) * QT],
                                               rsl, ab2[:, 0:QT], ALU.mult),
                           deps=[ae, wo_mm_last])
                    xn1_tok[q][mc] = P.emit(
                        "vector",
                        lambda e, mc=mc, q=q:
                            e.tensor_tensor(abuf[:, mc * TH + q * QT:mc * TH + (q + 1) * QT],
                                            abuf[:, mc * TH + q * QT:mc * TH + (q + 1) * QT],
                                            ab2[:, QT:E2], ALU.add),
                        deps=[be2], inc="dve")
                    n_last = xn1_tok[q][mc]
                ln_done[q] = [sx, sq_last, n_last]

            wo_mm_of = {}
            cw_done = None
            for q in range(NQ):
                amm = [wo_mms(q, mc) for mc in range(4)]
                wo_mm_of[q] = wo_last[0]
                if q > 0:
                    rows_norm(q - 1, wo_mm_of[q - 1])
                r_tok = [wo_rt(q, mc, amm[mc]) for mc in range(4)]
                if q > 0:
                    le = router_q_mm(h, q - 1, xn1_tok[q - 1])
                for mc in range(4, NCH):
                    mm = wo_mms(q, mc)
                    wo_mm_of[q] = mm
                    r_tok.append(wo_rt(q, mc, mm))
                    if q > 0 and mc == 5:
                        cwt = router_q_tr(le)
                    if q > 0 and mc == 7:
                        cw_done = router_q_out(h, q - 1, cwt)
                stats(q, r_tok)
            rows_norm(NQ - 1, wo_mm_of[NQ - 1])
            return xn1_tok, last_xt_read[0], wo_last[0], None

        def ln_rows_pre(sx, sx2, ps_sum, ps_sq):
            mt = P.emit("vector",
                        lambda e: e.tensor_scalar(R_m, ps_sum[0:1, 0:QT],
                                                  1.0 / D, None, ALU.mult),
                        deps=[sx, sx2], inc="dve")
            vt = P.emit("vector",
                        lambda e: e.tensor_scalar(R_v, ps_sq[0:1, 0:QT],
                                                  1.0 / D, None, ALU.mult),
                        deps=[sx2], inc="dve")
            if ps_sum is psS[1]:
                war["psS1a"] = [mt]
                war["psS0a"] = [vt]
            P.emit("vector", lambda e: e.tensor_tensor(R_t, R_m, R_m, ALU.mult),
                   deps=[mt])
            tv = P.emit("vector", lambda e: e.tensor_tensor(R_v, R_v, R_t, ALU.subtract),
                        deps=[vt], inc="dve")
            l1 = P.emit("scalar",
                        lambda e: e.activation(R_t, R_v, AF.Ln, bias=SC_eps),
                        deps=[tv], inc="act")
            at = P.emit("scalar",
                        lambda e: e.activation(R_a, R_t, AF.Exp, scale=-0.5),
                        deps=[l1], inc="act")
            ab_ = P.emit("vector", lambda e: e.tensor_copy(rbf[0:1, 0:QT], R_a),
                         deps=[at], inc="dve")
            P.emit("vector", lambda e: e.tensor_tensor(R_t, R_m, R_a, ALU.mult),
                   deps=[at])
            bb_ = P.emit("vector",
                         lambda e: e.tensor_scalar(rbf[0:1, QT:E2], R_t,
                                                   -1.0, None, ALU.mult),
                         inc="dve")
            return ab_, bb_

        def ln_rows_post(ab_, bb_, extra=()):
            am = P.emit("tensor",
                        lambda e: e.matmul(psM[:, 0:QT], ones_r[0:1, 0:128],
                                           rbf[0:1, 0:QT], start=True, stop=True),
                        deps=[ab_, war["psM"]] + list(extra), inc="pe")
            ae = P.emit("vector",
                        lambda e: e.tensor_copy(ab2[:, 0:QT], psM[:, 0:QT]),
                        deps=[am], inc="dve")
            bm2 = P.emit("tensor",
                         lambda e: e.matmul(psM[:, 0:QT], ones_r[0:1, 0:128],
                                            rbf[0:1, QT:E2], start=True, stop=True),
                         deps=[bb_, ae], inc="pe")
            be2 = P.emit("vector",
                         lambda e: e.tensor_copy(ab2[:, QT:E2], psM[:, 0:QT]),
                         deps=[bm2], inc="dve")
            war["psM"] = [be2]
            return be2, ae

        def ln_rows(sx, sx2, ps_sum, ps_sq):
            ab_, bb_ = ln_rows_pre(sx, sx2, ps_sum, ps_sq)
            be2, ae = ln_rows_post(ab_, bb_)
            return ab_, be2, ae

        def router_q_mm(h, q, xn1_q):
            lm = None
            for kc in range(NCH):
                lm = P.emit("tensor",
                            lambda e, kc=kc, q=q:
                                e.matmul(psM[0:16, 0:QT],
                                         wrb[:, kc * 16:(kc + 1) * 16],
                                         abuf[:, kc * TH + q * QT:kc * TH + (q + 1) * QT],
                                         start=(kc == 0), stop=(kc == NCH - 1)),
                            deps=[xn1_q[kc], tk_wr] + ([war["psM"]] if kc == 0 else []),
                            inc="pe" if kc == NCH - 1 else None)
            le = P.emit("vector",
                        lambda e: e.tensor_copy(xf32[0:16, 0:QT], psM[0:16, 0:QT]),
                        deps=[lm, war["xf"]], inc="dve")
            return le

        def router_q_tr(le):
            trt = None
            for st in range(NST):
                trt = P.emit("tensor",
                             lambda e, st=st:
                                 e.transpose(psM[:, st * 16:(st + 1) * 16],
                                             xf32[0:16, st * 128:(st + 1) * 128],
                                             cidfb[0:16, 0:16]),
                             deps=[le, tk_cidf], inc="pe")
            war["xf"] = [trt]
            pv = P.emit("vector",
                        lambda e: e.tensor_copy(prs[:, :], psM[:, 0:NST * 16]),
                        deps=[trt, war["astg"]], inc="dve")
            war["psM"] = [pv]
            p3 = prs.rearrange("p (a b) -> p a b", b=16)
            c3 = cwf.rearrange("p (a b) -> p a b", b=16)
            m4 = biasb[:, 4:4 + NST]
            d4 = biasb[:, 8:8 + NST]
            m4b = m4.rearrange("p (a o) -> p a o", o=1).broadcast_to([128, NST, 16])
            d4b = d4.rearrange("p (a o) -> p a o", o=1).broadcast_to([128, NST, 16])
            P.emit("vector",
                   lambda e: e.tensor_reduce(m4, p3, AX.X, ALU.max),
                   deps=[pv])
            P.emit("vector",
                   lambda e: e.tensor_tensor(c3, p3, m4b, ALU.subtract))
            exq = P.emit("scalar",
                         lambda e: e.activation(prs, cwf, AF.Exp),
                         deps=[(P.cnt["dve"] and ("dve", P.cnt["dve"]))],
                         inc="act")
            P.emit("vector",
                   lambda e: e.tensor_reduce(d4, p3, AX.X, ALU.add),
                   deps=[exq])
            P.emit("vector", lambda e: e.reciprocal(d4, d4))
            P.emit("vector", lambda e: e.tensor_tensor(c3, p3, d4b, ALU.mult))
            P.emit("vector", lambda e: e.tensor_reduce(m4, c3, AX.X, ALU.max))
            P.emit("vector", lambda e: e.tensor_tensor(p3, c3, m4b, ALU.is_ge))
            P.emit("vector", lambda e: e.tensor_scalar(prs, prs, -2.0, None, ALU.mult))
            P.emit("vector", lambda e: e.tensor_tensor(p3, c3, p3, ALU.add))
            P.emit("vector", lambda e: e.tensor_reduce(m4, p3, AX.X, ALU.max))
            P.emit("vector", lambda e: e.tensor_tensor(p3, c3, m4b, ALU.is_ge))
            cwt = P.emit("vector",
                         lambda e: e.tensor_tensor(cwf, cwf, prs, ALU.mult),
                         inc="dve")
            return cwt

        def router_q_out(h, q, cwt):
            cw_tok = [cwt] * NST
            trc = None
            for st in range(NST):
                trc = P.emit("tensor",
                             lambda e, st=st:
                                 e.transpose(psM[0:16, st * 128:(st + 1) * 128],
                                             cwf[:, st * 16:(st + 1) * 16],
                                             cidfb[:, :]),
                             deps=[cw_tok[st], war["psM"]], inc="pe")
            cw_done = P.emit("vector",
                             lambda e, q=q, h=h:
                                 e.tensor_copy(cwT[0:16, h * TH + q * QT:
                                                   h * TH + (q + 1) * QT],
                                               psM[0:16, 0:QT]),
                             deps=[trc], inc="dve")
            war["psM"] = [cw_done]
            war["astg"] = [cw_done]
            return cw_done

        def router_q(h, q, xn1_q):
            le = router_q_mm(h, q, xn1_q)
            cwt = router_q_tr(le)
            return router_q_out(h, q, cwt)

        def ffn_all(xsrcs, cw_dones, src_toks, accw):
            last_w_read = None
            """xsrcs[h](kc, q) -> AP of the xn1 slice for half h;
            src_toks[h][q] -> deps proving that slice is written (h0: the
            reload DMA, h1: the LN1 normalize tokens)."""
            tok_slot = [None, None]
            acc_tok = {}
            cwb_tok = [None] * NQ
            pend_ln = []

            def acc_ap(h, mc, q):
                if h == 1:
                    return xt[:, ACC + mc * TH + q * QT:ACC + mc * TH + (q + 1) * QT]
                return abuf[:, mc * TH + q * QT:mc * TH + (q + 1) * QT]

            acc3v = [abuf[:, 0:NCH * TH].rearrange("p (a t) -> p a t", a=NCH),
                     xt[:, ACC:ACC + NCH * TH].rearrange("p (a t) -> p a t", a=NCH)]
            xsrc3v = [xt[:, 0:NCH * TH].rearrange("p (a t) -> p a t", a=NCH),
                      abuf[:, 0:NCH * TH].rearrange("p (a t) -> p a t", a=NCH)]
            out3 = out_d.rearrange("p (a t) -> p a t", a=NCH)
            ost3 = [outst.rearrange("p (a t) -> p a t", a=2),
                    astg.rearrange("p (a t) -> p a t", a=2)]

            def flush_ln():
                while pend_ln:
                    ln2_q_tail(pend_ln.pop(0))

            pend_stats = []

            def flush_stats():
                while pend_stats:
                    h_, q_ = pend_stats.pop(0)
                    pend_ln.append(ln2_q_stats(h_, q_))

            def ln2_q_stats(h, q):
                r_tok = acc_tok[(h, q)]
                if h == 1:
                    accw[q] = [None] * NCH
                sx = sx2 = None
                for mc in range(NCH):
                    dep0 = [r_tok[mc]] + ([war["psV"]] if mc == 0 else [])
                    sx = P.emit("tensor",
                                lambda e, mc=mc, q=q, h=h:
                                    e.matmul(psV[0:1, 0:QT], ones_c[:, :],
                                             acc_ap(h, mc, q),
                                             start=(mc == 0), stop=(mc == NCH - 1)),
                                deps=dep0, inc="pe" if mc == NCH - 1 else None)
                    sqt = P.emit("scalar",
                                 lambda e, mc=mc, q=q, h=h:
                                     e.activation(sqk[:, (mc % 2) * QT:(mc % 2 + 1) * QT],
                                                  acc_ap(h, mc, q),
                                                  AF.Square),
                                 deps=[r_tok[mc], war["sqk%d" % (mc % 2)]],
                                 inc="act", nochain=True)
                    sx2 = P.emit("tensor",
                                 lambda e, mc=mc:
                                     e.matmul(psM[0:1, 0:QT], ones_c[:, :],
                                              sqk[:, (mc % 2) * QT:(mc % 2 + 1) * QT],
                                              start=(mc == 0),
                                              stop=(mc == NCH - 1)),
                                 deps=[sqt] + ([war["psM"]] if mc == 0 else []),
                                 inc="pe")
                    war["sqk%d" % (mc % 2)] = [sx2]
                ab_, bb_ = ln_rows_pre(sx, sx2, psV, psM)
                war["psV"] = [ab_]
                war["psM"] = [ab_]
                return (h, q, ab_, bb_)

            last_nts = [[]]

            def ln2_q_tail(st):
                h, q, ab_, bb_ = st
                be2, ae = ln_rows_post(ab_, bb_, extra=last_nts[0])
                a2b = ab2[:, 0:QT].rearrange("p (a t) -> p a t",
                                             a=1).broadcast_to([128, 2, QT])
                b2b = ab2[:, QT:E2].rearrange("p (a t) -> p a t",
                                              a=1).broadcast_to([128, 2, QT])
                nts = [None] * NCH
                for mp in range(NCH // 2):
                    sl = mp % 2
                    eng, qsem = ("gpsimd", "qw") if (h == 0 and sl == 0) \
                        else ("sync", "qx")
                    ml = P.emit("gpsimd",
                                lambda e, mp=mp, q=q, h=h, sl=sl, a2b=a2b:
                                    e.tensor_tensor(ost3[sl][:, :, :],
                                                    acc3v[h][:, 2 * mp:2 * mp + 2,
                                                             q * QT:(q + 1) * QT],
                                                    a2b, ALU.mult),
                                deps=[ae, st_outst[sl],
                                      war["astg"] if sl else None], inc="pl")
                    nt = P.emit("gpsimd",
                                lambda e, sl=sl, b2b=b2b:
                                    e.tensor_tensor(ost3[sl][:, :, :],
                                                    ost3[sl][:, :, :],
                                                    b2b, ALU.add),
                                deps=[be2, ml], inc="pl")
                    nts[2 * mp] = nts[2 * mp + 1] = nt
                    last_nts[0] = [nt]
                    ot = P.emit(eng,
                                lambda e, mp=mp, q=q, h=h, sl=sl:
                                    e.dma_start(out3[:, 2 * mp:2 * mp + 2,
                                                     h * TH + q * QT:
                                                     h * TH + (q + 1) * QT],
                                                ost3[sl][:, :, :]),
                                deps=[nt], inc=qsem)
                    st_outst[sl] = ot
                return nts

            def load_moe(si):
                sl = si % 2
                eo = EXP_ORDER[si % 18]
                tg = P.emit("gpsimd",
                            lambda e, sl=sl, eo=eo:
                                e.dma_start(wb[sl][:, 0:4096],
                                            moew[eo, :, 0:4096]),
                            deps=[moe_slot_war[sl]], inc="qw")
                tu = P.emit("gpsimd",
                            lambda e, sl=sl, eo=eo:
                                e.dma_start(wb[sl][:, 4096:8192],
                                            moew[eo, :, 4096:8192]),
                            inc="qw")
                td = P.emit("gpsimd",
                            lambda e, sl=sl, eo=eo:
                                e.dma_start(wb[sl][:, 8192:12288],
                                            moew[eo, :, 8192:12288]),
                            inc="qw")
                tok_slot[sl] = (tg, tu, td)

            load_moe(0)
            load_moe(1)
            def phase_A(ctx, q):
                h = ctx["h"]
                sl = ctx["sl"]
                e_orig = ctx["e_orig"]
                routed = ctx["routed"]
                my_tok = ctx["my_tok"]
                cw_done = cw_dones[h]
                hb = AHI + (q % 2) * (4 * QT)
                if routed:
                    cm = P.emit("tensor",
                                lambda e, eo=e_orig, q=q, h=h:
                                    e.matmul(psM[:, 0:QT],
                                             ceimb[0:16, eo * 128:(eo + 1) * 128],
                                             cwT[0:16, h * TH + q * QT:
                                                 h * TH + (q + 1) * QT],
                                             start=True, stop=True),
                                deps=[cw_done, war["psM"], tk_ceim], inc="pe")
                    cwb_tok[q] = P.emit(
                        "vector",
                        lambda e, q=q: e.tensor_copy(
                            qtb[:, q * QT:(q + 1) * QT], psM[:, 0:QT]),
                        deps=[cm, war["qtw"][q]], inc="dve", nochain=True)
                    war["psM"] = [cwb_tok[q]]
                    war["qtw"][q] = []
                h_tok = [None] * 4
                GB = [(psS[0], 0, "psS0a"), (psS[1], 0, "psS1a"),
                      (psV, 0, "psV")]
                UB = [(psS[0], QT, "psS0b"), (psS[1], QT, "psS1b")]
                for fc in range(4):
                    gp, go, ga = GB[fc % 3]
                    up, uo, gb = UB[fc % 2]
                    gm = None
                    for kc in range(NCH):
                        dep = ([my_tok[0]] + src_toks[h][q] if fc == 0 else []) \
                            + [war[ga]] if kc == 0 else []
                        gm = P.emit(
                            "tensor",
                            lambda e, sl=sl, kc=kc, fc=fc, q=q, gp=gp, go=go:
                                e.matmul(gp[:, go:go + QT],
                                         wb[sl][:, kc * 512 + fc * 128:
                                               kc * 512 + (fc + 1) * 128],
                                         xsrcs[h](kc, q),
                                         start=(kc == 0), stop=(kc == NCH - 1)),
                            deps=dep, inc="pe" if kc == NCH - 1 else None)
                    um = None
                    for kc in range(NCH):
                        dep = ([my_tok[1]] if fc == 0 else []) + [war[gb]] \
                            if kc == 0 else []
                        um = P.emit(
                            "tensor",
                            lambda e, sl=sl, kc=kc, fc=fc, q=q, up=up, uo=uo:
                                e.matmul(up[:, uo:uo + QT],
                                         wb[sl][:, 4096 + kc * 512 + fc * 128:
                                               4096 + kc * 512 + (fc + 1) * 128],
                                         xsrcs[h](kc, q),
                                         start=(kc == 0), stop=(kc == NCH - 1)),
                            deps=dep, inc="pe" if kc == NCH - 1 else None)
                    sl_t = P.emit("scalar",
                                  lambda e, gp=gp, go=go:
                                      e.activation(sq[:, :], gp[:, go:go + QT],
                                                   AF.Silu),
                                  deps=[gm, war["sq"]], inc="act", nochain=True)
                    war[ga] = [sl_t]
                    hm = P.emit("vector",
                                lambda e, fc=fc, up=up, uo=uo, hb=hb:
                                    e.tensor_tensor(
                                        abuf[:, hb + fc * QT:hb + (fc + 1) * QT],
                                        up[:, uo:uo + QT], sq[:, :], ALU.mult),
                                deps=[um, sl_t, war["H%d" % (q % 2)]],
                                inc="dve", nochain=True)
                    war[gb] = [hm]
                    war["sq"] = [hm]
                    if routed:
                        hm = P.emit(
                            "gpsimd",
                            lambda e, fc=fc, q=q, hb=hb:
                                e.tensor_tensor(
                                    abuf[:, hb + fc * QT:hb + (fc + 1) * QT],
                                    abuf[:, hb + fc * QT:hb + (fc + 1) * QT],
                                    qtb[:, q * QT:(q + 1) * QT],
                                    ALU.mult),
                            deps=[cwb_tok[q], hm], inc="pl")
                    h_tok[fc] = hm
                if routed:
                    war["qtw"][q] = [h_tok[3]]
                return h_tok

            def phase_B(ctx, q, h_tok):
                nonlocal last_w_read
                flush_stats()
                ei = ctx["ei"]
                sl = ctx["sl"]
                h = ctx["h"]
                hb = AHI + (q % 2) * (4 * QT)
                dm = None
                for mc in range(NCH):
                    pa = psA[mc % 2]
                    pan = "psA%d" % (mc % 2)
                    for fc in range(4):
                        dep = [h_tok[fc]] + ([war[pan]] if fc == 0 else []) \
                            + ([ctx["my_tok"][2]] if mc == 0 and fc == 0 else [])
                        dm = P.emit(
                            "tensor",
                            lambda e, sl=sl, fc=fc, mc=mc, pa=pa, hb=hb:
                                e.matmul(pa[:, 0:QT],
                                         wb[sl][:, 8192 + fc * 1024 + mc * 128:
                                               8192 + fc * 1024 + (mc + 1) * 128],
                                         abuf[:, hb + fc * QT:hb + (fc + 1) * QT],
                                         start=(fc == 0),
                                         stop=(fc == 3 and ei != 17)),
                            deps=dep, inc="pe" if (fc == 3 and ei != 17) else None)
                    if ei == 17:
                        dm = P.emit(
                            "tensor",
                            lambda e, mc=mc, q=q, h=h, pa=pa:
                                e.matmul(pa[:, 0:QT], cidb[:, :],
                                         xsrcs[h](mc, q),
                                         start=False, stop=True),
                            deps=[tk_cid], inc="pe")
                    last_w_read = dm
                    if ei == 0:
                        ac = P.emit(
                            "scalar",
                            lambda e, mc=mc, q=q, pa=pa, h=h:
                                e.activation(acc_ap(h, mc, q), pa[:, 0:QT],
                                             AF.Copy),
                            deps=[dm, accw[q][mc]], inc="act", nochain=True)
                        acc_tok.setdefault((h, q), [None] * NCH)
                    else:
                        ac = P.emit(
                            "vector",
                            lambda e, mc=mc, q=q, pa=pa, h=h:
                                e.tensor_tensor(
                                    acc_ap(h, mc, q),
                                    pa[:, 0:QT],
                                    acc_ap(h, mc, q), ALU.add),
                            deps=[dm, acc_tok[(h, q)][mc]], inc="dve")
                    acc_tok[(h, q)][mc] = ac
                    war[pan] = [ac]
                war["H%d" % (q % 2)] = [dm]
                flush_ln()
                if ei == 17:
                    pend_stats.append((h, q))

            seq = [(si, q) for si in range(36) for q in range(NQ)]

            def make_ctx(si):
                return {"si": si, "h": 1 - si // 18, "ei": si % 18, "sl": si % 2,
                        "e_orig": EXP_ORDER[si % 18],
                        "routed": EXP_ORDER[si % 18] < 16,
                        "my_tok": tok_slot[si % 2]}

            ctxs = {0: make_ctx(0)}
            ht = {}
            for i, (si, q) in enumerate(seq):
                if i == 0:
                    ht[(si, q)] = phase_A(ctxs[0], q)
                    if callable(cw_dones[1]):
                        cw_dones[1] = cw_dones[1]()
                if i + 1 < len(seq):
                    nsi, nq2 = seq[i + 1]
                    if nsi not in ctxs:
                        ctxs[nsi] = make_ctx(nsi)
                    ht[seq[i + 1]] = phase_A(ctxs[nsi], nq2)
                phase_B(ctxs[si], q, ht.pop((si, q)))
                if q == NQ - 1:
                    moe_slot_war[si % 2] = [last_w_read]
                    if si + 2 < 36:
                        load_moe(si + 2)
                    ctxs.pop(si, None)
            flush_stats()
            flush_ln()

        def stop_here():
            allc = [("pe", P.cnt["pe"]), ("act", P.cnt["act"]),
                    ("dve", P.cnt["dve"]), ("qw", P.cnt["qw"])]
            P.emit("gpsimd",
                   lambda e: e.dma_start(out_d[0:128, 0:NCH * TH],
                                         abuf[:, 0:NCH * TH]),
                   deps=allc, inc="qw")
            P.emit("gpsimd",
                   lambda e: e.dma_start(out_d[0:128, NCH * TH:2 * NCH * TH],
                                         xt[:, 0:NCH * TH]),
                   deps=allc, inc="qw")
            P.wait_only("sync", [("qx", P.cnt["qx"]), ("qw", P.cnt["qw"]),
                                 ("qs", P.cnt["qs"])])

        if STAGE >= 1:
            a0, s_mm0, pv_mm0 = attention_half(0, None, next_qkv_slot=0)
            tk_wo = P.emit("gpsimd",
                           lambda e: e.dma_start(wb[1][:, 0:8192], woi),
                           inc="qw")
        if STAGE >= 2:
            xn1_h0, xt_read0, wo0, _ = wo_ln1_half(0, a0)
        if STAGE >= 3:
            sp = P.emit("scalar",
                        lambda e: e.dma_start(scr, abuf[:, 0:NCH * TH]),
                        deps=[[t for row in xn1_h0 for t in row]], inc="qs",
                        nochain=True)
            abuf_free = [sp]
            cw_box = [None]

            def pend0():
                le0 = router_q_mm(0, NQ - 1, xn1_h0[NQ - 1])
                cwt0 = router_q_tr(le0)
                cw_box[0] = router_q_out(0, NQ - 1, cwt0)
                abuf_free.append(le0)
        if STAGE >= 4:
            kvn = war.get("kv_next")
            a1, s_mm1, pv_mm1 = attention_half(1, abuf_free,
                                               pend_start=pend0)
        if STAGE >= 5:
            xn1_h1, xt_read1, wo1, _ = wo_ln1_half(1, a1)
            cw0 = cw_box[0]
            cw1 = lambda: router_q(1, NQ - 1, xn1_h1[NQ - 1])
            rl = P.emit("scalar",
                        lambda e: e.dma_start(xt[:, 0:NCH * TH], scr),
                        deps=[xt_read1, sp], inc="qs", nochain=True)
            moe_slot_war[0] = [pv_mm1, qkv_war[0], qkv_war[1]]
            moe_slot_war[1] = [s_mm1, wo1]
        if STAGE >= 6:
            accw0 = [[xt_read1] * NCH for _ in range(NQ)]
            ffn_all(
                [lambda kc, q: xt[:, kc * TH + q * QT:kc * TH + (q + 1) * QT],
                 lambda kc, q: abuf[:, kc * TH + q * QT:kc * TH + (q + 1) * QT]],
                [cw0, cw1],
                [[[rl]] * NQ, [list(xn1_h1[q]) for q in range(NQ)]],
                accw0)
            P.wait_only("sync", [("qx", P.cnt["qx"]), ("qw", P.cnt["qw"]),
                                 ("qs", P.cnt["qs"])])
        else:
            stop_here()

        sems = {s: ctx.enter_context(nc.semaphore(s)) for s in SEMS}
        block = ctx.enter_context(nc.Block())

        def player(eng):
            def run(e):
                with nc.allow_low_precision(reason="bf16 stream by design"):
                    for waits, fn, inc, n in P.st[eng]:
                        for s, v in waits:
                            e.wait_ge(sems[s], v)
                        if fn is None:
                            continue
                        inst = fn(e)
                        if inc is not None:
                            inst.then_inc(sems[inc], n)
            return run

        block.tensor(player("tensor"))
        block.scalar(player("scalar"))
        block.vector(player("vector"))
        block.gpsimd(player("gpsimd"))
        block.sync(player("sync"))

    return nc, {e: len(P.st[e]) for e in ENGS}


def _img_chunks(w):
    K, M = w.shape
    return np.ascontiguousarray(
        w.reshape(K // 128, 128, M).transpose(1, 0, 2).reshape(128, (K // 128) * M))


def prep_inputs(L0_attn_x, L0_attn_Wq, L0_attn_Wk, L0_attn_Wv, L0_attn_Wo,
                L0_attn_bo, L0_ffn_Wr, L0_ffn_Wg, L0_ffn_Wu, L0_ffn_Wd,
                L0_ffn_Sg, L0_ffn_Su, L0_ffn_Sd):
    f = np.float32
    x = np.asarray(L0_attn_x, f)
    Wq = np.asarray(L0_attn_Wq, f); Wk = np.asarray(L0_attn_Wk, f)
    Wv = np.asarray(L0_attn_Wv, f); Wo = np.asarray(L0_attn_Wo, f)
    bo = np.asarray(L0_attn_bo, f)
    Wr = np.asarray(L0_ffn_Wr, f)
    Wg = np.asarray(L0_ffn_Wg, f); Wu = np.asarray(L0_ffn_Wu, f)
    Wd = np.asarray(L0_ffn_Wd, f)
    Sg = np.asarray(L0_ffn_Sg, f); Su = np.asarray(L0_ffn_Su, f)
    Sd = np.asarray(L0_ffn_Sd, f)

    def pair_img(W):
        return W.reshape(8, 128, 8, 128).transpose(2, 1, 0, 3).reshape(8, 128, 1024)

    attnw = np.ascontiguousarray(
        np.concatenate([pair_img(Wq), pair_img(Wk), pair_img(Wv)], axis=2))
    woi = _img_chunks(Wo)
    boi = np.ascontiguousarray(bo.reshape(1, 1024))
    wri = _img_chunks(Wr)
    moe = []
    for e in range(16):
        moe.append(np.concatenate(
            [_img_chunks(Wg[e]), _img_chunks(Wu[e]), _img_chunks(Wd[e])], axis=1))
    for half in range(2):
        sl = slice(512 * half, 512 * (half + 1))
        moe.append(np.concatenate(
            [_img_chunks(Sg[:, sl]), _img_chunks(Su[:, sl]),
             _img_chunks(np.ascontiguousarray(Sd[sl, :]))], axis=1))
    moew = np.ascontiguousarray(np.stack(moe))

    cid = np.eye(128, dtype=bf16)
    cidf = np.eye(128, dtype=f)
    ceim = np.zeros((16, 2048), dtype=bf16)
    for e in range(16):
        ceim[e, e * 128:(e + 1) * 128] = bf16(1.0)

    shared = dict(attnw=attnw, woi=woi, boi=boi, wri=wri, moew=moew,
                  cid=cid, cidf=cidf, ceim=ceim)
    in_maps = []
    for c in range(x.shape[0]):
        xi = np.ascontiguousarray(
            x[c].reshape(T, 8, 128).transpose(2, 1, 0).reshape(128, 8 * T))
        m = dict(shared)
        m["xti"] = xi
        in_maps.append(m)
    return in_maps


def kernel(**inputs):
    in_maps = prep_inputs(**inputs)
    nc, _counts = build_program()
    res = run_bass_kernel_spmd(nc, in_maps, core_ids=list(range(8)))
    outs = []
    for r in res.results:
        img = np.asarray(r["out"])
        y = img.reshape(128, 8, T).transpose(2, 1, 0).reshape(T, 1024)
        outs.append(y)
    return np.stack(outs).astype(np.float32)


if __name__ == "__main__":
    nc, counts = build_program()
    print("instruction counts:", counts)


# version 55
# speedup vs baseline: 1.0578x; 1.0578x over previous
import numpy as np
import ml_dtypes
from contextlib import ExitStack
import concourse.bass as bass
import concourse.mybir as mybir
from concourse.alu_op_type import AluOpType as ALU
from concourse.bass_utils import run_bass_kernel_spmd

bf16 = ml_dtypes.bfloat16
F32 = mybir.dt.float32
BF16 = mybir.dt.bfloat16
AF = mybir.ActivationFunctionType
AX = mybir.AxisListType

T = 4096
QT = 512
D = 1024
NCH = 8
NP = 8
EXP_ORDER = [16, 17] + list(range(16))
STAGE = 99
LIMIT_PAIRS = None
LIMIT_QT = None
SKIP_K = False
SKIP_V = False
SKIP_KNORM = False
SKIP_KEVAC = False

ENGS = ("tensor", "scalar", "vector", "gpsimd", "sync")
OWN = {"tensor": "pe", "scalar": "act", "vector": "dve"}
SEMS = ("pe", "act", "dve", "qw", "qx", "qs", "pl")


def _flat(deps):
    for d in deps:
        if d is None:
            continue
        if isinstance(d, list):
            yield from _flat(d)
        else:
            yield d


class Prog:

    def __init__(self):
        self.st = {e: [] for e in ENGS}
        self.cnt = {s: 0 for s in SEMS}
        self.lastw = {e: {} for e in ENGS}

    def emit(self, eng, fn, deps=(), inc=None, nochain=False):
        own = OWN.get(eng)
        if eng in ("vector", "scalar"):
            if not nochain:
                deps = list(deps) + [(own, self.cnt[own])]
            if inc is None and fn is not None:
                inc = own
        n = 16 if inc in ("qw", "qx", "qs") else 1
        if n == 16:
            deps = list(deps) + [(inc, self.cnt[inc])]
        waits = []
        seen = {}
        for s, v in _flat(deps):
            if s == "pe" and eng == "tensor" and v <= self.cnt[s]:
                continue
            if s in ("qw", "qx", "qs"):
                if v > seen.get(s, 0):
                    seen[s] = v
            elif v > self.lastw[eng].get(s, 0):
                seen[s] = max(seen.get(s, 0), v)
                self.lastw[eng][s] = v
        waits = sorted(seen.items())
        tok = None
        if inc is not None:
            self.cnt[inc] += n
            tok = (inc, self.cnt[inc])
        self.st[eng].append((waits, fn, inc, n))
        return tok

    def wait_only(self, eng, deps):
        self.emit(eng, None, deps=deps)


def build_program():
    nc = bass.Bass("TRN2", target_bir_lowering=False)
    TH = T // 2
    NQ = TH // QT
    NJ = T // 128
    NKT = T // QT
    E2 = 2 * QT
    NST = QT // 128
    AHI = NP * TH
    CWB = AHI + 4 * QT
    ABW = AHI + max(NCH * QT, 4 * QT + TH)
    ACC = NCH * TH

    xti = nc.dram_tensor("xti", [128, NCH * T], F32, kind="ExternalInput").ap()
    attnw = nc.dram_tensor("attnw", [NP, 128, 3072], F32, kind="ExternalInput").ap()
    woi = nc.dram_tensor("woi", [128, 8192], F32, kind="ExternalInput").ap()
    boi = nc.dram_tensor("boi", [1, 1024], F32, kind="ExternalInput").ap()
    wri = nc.dram_tensor("wri", [128, 128], F32, kind="ExternalInput").ap()
    moew = nc.dram_tensor("moew", [18, 128, 12288], F32, kind="ExternalInput").ap()
    cid = nc.dram_tensor("cid", [128, 128], BF16, kind="ExternalInput").ap()
    cidf = nc.dram_tensor("cidf", [128, 128], F32, kind="ExternalInput").ap()
    ceim = nc.dram_tensor("ceim", [16, 2048], BF16, kind="ExternalInput").ap()
    scr = nc.dram_tensor("scr", [128, NCH * TH], BF16, kind="Internal").ap()
    scr2 = nc.dram_tensor("scr2", [NP, 128, T + 130 * (T // 128)], BF16, kind="Internal").ap()
    out_d = nc.dram_tensor("out", [128, NCH * T], F32, kind="ExternalOutput").ap()

    P = Prog()
    ctx = ExitStack()
    sbt = lambda name, shape, dt: ctx.enter_context(nc.sbuf_tensor(name, shape, dt))
    pst = lambda name, shape, dt: ctx.enter_context(nc.psum_tensor(name, shape, dt))

    with ctx:
        xt = sbt("xt", [128, NCH * T], BF16)
        abuf = sbt("abuf", [128, ABW], BF16)
        wb = [sbt("wbA", [128, 12288], BF16), sbt("wbB", [128, 12288], BF16)]
        qtb = sbt("qtb", [128, 4 * QT], BF16)
        pt = sbt("pt", [128, 3 * E2], BF16)
        sq = sbt("sq", [128, QT], BF16)
        sqk = sbt("sqk", [128, 2 * QT], BF16)
        ab2 = sbt("ab2", [128, 2 * QT], BF16)
        cwT = sbt("cwT", [16, T], BF16)
        rowsA = sbt("rowsA", [128, 4 * QT + 24], F32)
        rbf = sbt("rbf", [1, 2 * QT], BF16)
        nbf = sbt("nbf", [1, 8], BF16)
        bob = sbt("bob", [1, 1024], BF16)
        biasb = sbt("biasb", [128, 12], F32)
        recipb = sbt("recipb", [64, 2 * QT], F32)
        astg = sbt("astg", [128, 2 * QT], F32)
        prs = astg[:, QT:QT + NST * 16]
        cwf = astg[:, QT + NST * 16:QT + 2 * NST * 16]
        outst = sbt("outst", [128, 2 * QT], F32)
        wrb = sbt("wrb", [128, 128], BF16)
        xf32 = sbt("xf32", [128, QT], F32)
        cidb = sbt("cidb", [128, 128], BF16)
        cidfb = sbt("cidfb", [128, 128], F32)
        ceimb = sbt("ceimb", [16, 2048], BF16)
        ones_c = sbt("ones_c", [128, 1], BF16)
        ones_r = sbt("ones_r", [1, QT], BF16)
        ones_f = sbt("ones_f", [128, 64], F32)

        psS = [pst("psS0", [128, E2], F32), pst("psS1", [128, E2], F32)]
        psA = [pst("psA0", [128, QT], F32), pst("psA1", [128, QT], F32)]
        psM = pst("psM", [128, QT], F32)
        psV = pst("psV", [128, QT], F32)

        kt_v = wb[1][:, 8192:8192 + T]
        va_v = wb[0][:, 6144:6144 + NJ * 130]

        R_m = rowsA[0:1, 0:QT]
        R_v = rowsA[0:1, QT:2 * QT]
        R_t = rowsA[0:1, 2 * QT:3 * QT]
        R_a = rowsA[0:1, 3 * QT:4 * QT]
        SC_kn = rowsA[0:1, 4 * QT:4 * QT + 1]
        SC_q = rowsA[0:1, 4 * QT + 1:4 * QT + 2]
        SC_c = rowsA[0:1, 4 * QT + 2:4 * QT + 3]
        SC_t = rowsA[0:1, 4 * QT + 3:4 * QT + 4]
        SC_eps = rowsA[0:1, 4 * QT + 4:4 * QT + 5]
        KSLOT = lambda p: rowsA[0:1, 4 * QT + 5 + p:4 * QT + 6 + p]
        SC_q4 = [rowsA[0:1, 4 * QT + 13 + i:4 * QT + 14 + i] for i in range(4)]
        SC_t4 = [rowsA[0:1, 4 * QT + 17 + i:4 * QT + 18 + i] for i in range(4)]

        tk_cid = P.emit("sync", lambda e: e.dma_start(cidb[:, :], cid), inc="qx")
        tk_cidf = P.emit("sync", lambda e: e.dma_start(cidfb[:, :], cidf), inc="qx")
        tk_ceim = P.emit("sync", lambda e: e.dma_start(ceimb[:, :], ceim), inc="qx")
        tk_wr = P.emit("gpsimd", lambda e: e.dma_start(wrb[:, :], wri), inc="qw")
        qkv_tok0 = P.emit("gpsimd",
                          lambda e: e.dma_start(wb[0][:, 0:3072], attnw[0, :, :]),
                          inc="qw")
        qkv_tok1 = P.emit("gpsimd",
                          lambda e: e.dma_start(wb[0][:, 3072:6144], attnw[1, :, :]),
                          inc="qw")
        xt3 = xt[:, :].rearrange("p (c t) -> p c t", c=NCH)
        xti3 = xti.rearrange("p (c t) -> p c t", c=NCH)
        P.emit("vector", lambda e: e.memset(SC_eps, 1e-5))
        P.emit("vector", lambda e: e.memset(ones_c[:, :], 1.0))
        P.emit("vector", lambda e: e.memset(ones_r[:, :], 1.0))
        P.emit("vector", lambda e: e.memset(ones_f[:, :], 1.0))
        va3 = va_v.rearrange("p (a b) -> p a b", b=130)
        P.emit("vector", lambda e, v=va3: e.memset(v[:, :, 64:65], 1.0))
        tk_ones = P.emit("vector", lambda e, v=va3: e.memset(v[:, :, 129:130], 1.0),
                         inc="dve")
        tk_xts = []
        for ki in range(T // QT):
            tk_xts.append(P.emit(
                "gpsimd",
                lambda e, ki=ki: e.dma_start(xt3[:, :, ki * QT:(ki + 1) * QT],
                                             xti3[:, :, ki * QT:(ki + 1) * QT]),
                inc="qw"))
        tk_xt = tk_xts[-1]
        tk_bo = P.emit("gpsimd", lambda e: e.dma_start(bob[:, :], boi), inc="qw")

        qkv_tok = [None, None]
        qkv_war = [[tk_ones], [tk_ones]]

        def load_qkv(p):
            s = p % 2
            qkv_tok[s] = P.emit(
                "gpsimd",
                lambda e, p=p, s=s: e.dma_start(
                    wb[0][:, 3072 * s:3072 * s + 3072], attnw[p, :, :]),
                deps=[qkv_war[s]], inc="qw")

        qkv_tok[0] = qkv_tok0
        qkv_tok[1] = qkv_tok1

        def qkvw(s, off, kc):
            return wb[0][:, 3072 * s + off + kc * 128:3072 * s + off + (kc + 1) * 128]

        war = {
            "psS0a": [tk_ones], "psS0b": [tk_ones],
            "psS1a": [tk_ones], "psS1b": [tk_ones],
            "psA0": [], "psA1": [], "psM": [], "psT": [],
            "kt": [], "va": [tk_ones],
            "qtw": [[], [], [], []], "biasw": [[], [], [], []], "astg": [],
            "sq": [], "sqk0": [], "sqk1": [], "psV": [],
            "xf": [], "H0": [], "H1": [],
        }
        st_outst = [None, None]
        moe_slot_war = [[], []]
        kvbank = [(psM, "psM")]

        def attention_half(h, abuf_free_tok, next_qkv_slot=None,
                           pend_start=None):
            s_last = [None]
            pv_last = [None]
            a_last = None
            final_aw = [None] * NQ
            kv_state = {}
            next_chains = {}
            pend_tail = [None]
            pend_cp = [None]
            cps_box = [None]

            def k_group(s, ki, kst):
                pb, pw = kvbank[0]
                kvbank[0] = (psV, "psV") if pw == "psM" else (psM, "psM")
                mm = None
                for kc in range(NCH):
                    dep = [qkv_tok[s], war[pw], tk_xts[ki]] if kc == 0 else []
                    mm = P.emit(
                        "tensor",
                        lambda e, s=s, kc=kc, ki=ki, pb=pb:
                            e.matmul(pb[:, 0:QT],
                                     qkvw(s, 1024, kc),
                                     xt[:, kc * T + ki * QT:kc * T + (ki + 1) * QT],
                                     start=(kc == 0), stop=(kc == NCH - 1)),
                        deps=dep, inc="pe" if kc == NCH - 1 else None)
                ev = P.emit("vector",
                            lambda e, ki=ki, pb=pb:
                                e.tensor_copy(kt_v[:, ki * QT:(ki + 1) * QT],
                                              pb[:, 0:QT]),
                            deps=[mm, war["kt"]], inc="dve", nochain=True)
                kst["kt_ready"] = ev
                war[pw] = [ev]
                sqt = P.emit("vector",
                             lambda e, ki=ki:
                                 e.tensor_tensor(sqk[:, (ki % 2) * QT:(ki % 2 + 1) * QT],
                                                 kt_v[:, ki * QT:(ki + 1) * QT],
                                                 kt_v[:, ki * QT:(ki + 1) * QT],
                                                 ALU.mult),
                             deps=[ev, war["sqk%d" % (ki % 2)]], inc="dve",
                             nochain=True)
                kst.setdefault("pend", []).append((ki, sqt))

            def knorm_one(kst):
                ki, sqt = kst["pend"].pop(0)
                knm = P.emit("tensor",
                             lambda e, ki=ki:
                                 e.matmul(psM[0:1, 0:QT], ones_c[:, :],
                                          sqk[:, (ki % 2) * QT:(ki % 2 + 1) * QT],
                                          start=True, stop=True),
                             deps=[sqt, war["psM"]], inc="pe")
                war["sqk%d" % (ki % 2)] = [knm]
                rm = P.emit("vector",
                            lambda e: e.tensor_reduce(SC_t, psM[0:1, 0:QT],
                                                      AX.X, ALU.max),
                            deps=[knm], inc="dve")
                war["psM"] = [rm]
                ks = KSLOT(kst["pidx"])
                if ki == 0:
                    kst["kmax"] = P.emit("vector",
                                         lambda e, ks=ks: e.tensor_copy(ks, SC_t),
                                         deps=[rm], inc="dve")
                else:
                    kst["kmax"] = P.emit("vector",
                                         lambda e, ks=ks:
                                             e.tensor_tensor(ks, ks, SC_t, ALU.max),
                                         deps=[rm], inc="dve")

            def knorm_batch(kst):
                while kst.get("pend"):
                    knorm_one(kst)
                kst.pop("pend", None)

            def v_group(s, ki, kst):
                pv_, pw = kvbank[0]
                kvbank[0] = (psV, "psV") if pw == "psM" else (psM, "psM")
                mm = None
                for jt4 in range(QT // 128):
                    jt = ki * (QT // 128) + jt4
                    for kc in range(NCH):
                        dep = [qkv_tok[s], war[pw], tk_xts[ki]] \
                            if (jt4 == 0 and kc == 0) else []
                        mm = P.emit(
                            "tensor",
                            lambda e, s=s, kc=kc, jt=jt, jt4=jt4, pv_=pv_:
                                e.matmul(pv_[:, jt4 * 128:(jt4 + 1) * 128],
                                         xt[:, kc * T + jt * 128:kc * T + (jt + 1) * 128],
                                         qkvw(s, 2048, kc),
                                         start=(kc == 0), stop=(kc == NCH - 1)),
                            deps=dep, inc="pe" if kc == NCH - 1 else None)
                c0 = P.emit(
                    "vector",
                    lambda e, ki=ki, pv_=pv_:
                        e.tensor_copy(
                            va3[:, ki * 4:(ki + 1) * 4, 0:64],
                            pv_[:, :].rearrange("p (a b) -> p a b", b=128)[:, :, 0:64]),
                    deps=[mm, war["va"]], inc="dve", nochain=True)
                c1 = P.emit(
                    "vector",
                    lambda e, ki=ki, pv_=pv_:
                        e.tensor_copy(
                            va3[:, ki * 4:(ki + 1) * 4, 65:129],
                            pv_[:, :].rearrange("p (a b) -> p a b", b=128)[:, :, 64:128]),
                    deps=[mm], inc="dve", nochain=True)
                war[pw] = [c0, c1]
                kst["va_ready"] = [c0, c1]

            def kv_full(s, pidx):
                kst = {"pidx": pidx}
                for ki in range(NKT):
                    k_group(s, ki, kst)
                    v_group(s, ki, kst)
                    knorm_one(kst)
                knorm_batch(kst)
                return kst

            def kv_load_slice(pnext, ki, kst, dep):
                lk = P.emit("sync",
                            lambda e, pnext=pnext, ki=ki:
                                e.dma_start(kt_v[:, ki * QT:(ki + 1) * QT],
                                            scr2[pnext, :, ki * QT:(ki + 1) * QT]),
                            deps=[dep], inc="qx")
                nv = (QT // 128) * 130
                lv = P.emit("sync",
                            lambda e, pnext=pnext, ki=ki, nv=nv:
                                e.dma_start(va_v[:, ki * nv:(ki + 1) * nv],
                                            scr2[pnext, :, T + ki * nv:
                                                 T + (ki + 1) * nv]),
                            deps=[dep], inc="qx")
                kst["kt_ready"] = lk
                kst["va_ready"] = lv

            serial_kv = LIMIT_PAIRS is not None or LIMIT_QT is not None
            npair = NP if LIMIT_PAIRS is None else LIMIT_PAIRS
            for p in range(npair):
                s = p % 2
                if p + 1 < NP:
                    if not (h == 0 and p == 0):
                        load_qkv(p + 1)
                elif h == 0:
                    load_qkv(0)
                if p == 0 and h == 1 and war.get("kv_next") is not None:
                    kst = war.pop("kv_next")
                elif p in kv_state:
                    kst = kv_state.pop(p)
                else:
                    kst = kv_full(s, p)
                kmax_tok = kst["kmax"]
                kt_ready = kst["kt_ready"]
                va_ready = kst["va_ready"]
                if h == 0 and not serial_kv:
                    spk = P.emit("sync",
                                 lambda e, p=p: e.dma_start(scr2[p, :, 0:T],
                                                            kt_v[:, :]),
                                 deps=[kt_ready, s_last[0]], inc="qx")
                    spv = P.emit("sync",
                                 lambda e, p=p:
                                     e.dma_start(scr2[p, :, T:T + NJ * 130],
                                                 va_v[:, :]),
                                 deps=[va_ready, pv_last[0]], inc="qx")
                    war["kt"] = war["kt"] + [spk]
                    war["va"] = war["va"] + [spv]

                def next_slot():
                    if serial_kv:
                        return None
                    if p + 1 < npair:
                        return (p + 1, (p + 1) % 2)
                    if h == 0 and next_qkv_slot is not None:
                        return ("next_half", 0)
                    return None

                qtw = war["qtw"]
                biasw = war["biasw"]

                def q_chain_A(q, sx_=None):
                    ss = s if sx_ is None else sx_
                    sl = q % 4
                    pb, pw = (psM, "psM") if q % 2 == 0 else (psV, "psV")
                    tq0 = h * TH + q * QT
                    qmm = None
                    for kc in range(NCH):
                        dep = [war[pw], qkv_tok[ss],
                               tk_xts[(h * TH + q * QT) // QT]] if kc == 0 else []
                        qmm = P.emit(
                            "tensor",
                            lambda e, ss=ss, kc=kc, tq0=tq0, pb=pb:
                                e.matmul(pb[:, 0:QT],
                                         qkvw(ss, 0, kc),
                                         xt[:, kc * T + tq0:kc * T + tq0 + QT],
                                         start=(kc == 0), stop=(kc == NCH - 1)),
                            deps=dep, inc="pe" if kc == NCH - 1 else None)
                    qkv_war[ss] = [qmm]
                    qe = P.emit("vector",
                                lambda e, sl=sl, pb=pb:
                                    e.tensor_scalar(qtb[:, sl * QT:(sl + 1) * QT],
                                                    pb[:, 0:QT], 0.125, None,
                                                    ALU.mult),
                                deps=[qmm, qtw[sl]], inc="dve", nochain=True)
                    war[pw] = [qe]
                    return qe

                def q_chain_B(q, qe):
                    sl = q % 4
                    pb, pw = (psM, "psM") if q % 2 == 0 else (psV, "psV")
                    qs = P.emit("vector",
                                lambda e, sl=sl, q=q:
                                    e.tensor_tensor(sqk[:, (q % 2) * QT:(q % 2 + 1) * QT],
                                                    qtb[:, sl * QT:(sl + 1) * QT],
                                                    qtb[:, sl * QT:(sl + 1) * QT],
                                                    ALU.mult),
                                deps=[qe, war["sqk%d" % (q % 2)]], inc="dve",
                                nochain=True)
                    qn = P.emit("tensor",
                                lambda e, q=q: e.matmul(pb[0:1, 0:QT], ones_c[:, :],
                                                        sqk[:, (q % 2) * QT:
                                                            (q % 2 + 1) * QT],
                                                        start=True, stop=True)
                                if False else
                                    e.matmul(pb[0:1, 0:QT], ones_c[:, :],
                                             sqk[:, (q % 2) * QT:(q % 2 + 1) * QT],
                                             start=True, stop=True),
                                deps=[qs, war[pw]], inc="pe")
                    war["sqk%d" % (q % 2)] = [qn]
                    qm = P.emit("vector",
                                lambda e, pb=pb: e.tensor_reduce(
                                    SC_q4[q], pb[0:1, 0:QT], AX.X, ALU.max),
                                deps=[qn], inc="dve")
                    war[pw] = [qm]
                    return qm

                def q_chain_C_all(qms, nq):
                    sc4 = rowsA[0:1, 4 * QT + 13:4 * QT + 13 + nq]
                    st4 = rowsA[0:1, 4 * QT + 17:4 * QT + 17 + nq]
                    s3 = sc4.rearrange("p (a b) -> p a b", b=1)
                    k3 = KSLOT(p).rearrange("p (a b) -> p a b", b=1) \
                        .broadcast_to([1, nq, 1])
                    cc = P.emit("vector",
                                lambda e, s3=s3, k3=k3:
                                    e.tensor_tensor(s3, s3, k3, ALU.mult),
                                deps=[qms, kmax_tok], inc="dve")
                    l1 = P.emit("scalar",
                                lambda e, sc4=sc4, st4=st4:
                                    e.activation(st4, sc4, AF.Ln),
                                deps=[cc], inc="act")
                    e1 = P.emit("scalar",
                                lambda e, sc4=sc4, st4=st4:
                                    e.activation(sc4, st4, AF.Exp, scale=0.5),
                                deps=[l1], inc="act")
                    ng = P.emit("vector",
                                lambda e, sc4=sc4, nq=nq:
                                    e.tensor_scalar(nbf[0:1, 0:nq], sc4,
                                                    -1.0, None, ALU.mult),
                                deps=[e1], inc="dve")
                    bc = P.emit("tensor",
                                lambda e, nq=nq:
                                    e.matmul(psM[:, 0:nq], ones_r[0:1, 0:128],
                                             nbf[0:1, 0:nq],
                                             start=True, stop=True),
                                deps=[ng, war["psM"]], inc="pe")
                    be = P.emit("vector",
                                lambda e, nq=nq:
                                    e.tensor_copy(biasb[:, 0:nq], psM[:, 0:nq]),
                                deps=[bc] + [biasw[i] for i in range(nq)],
                                inc="dve", nochain=True)
                    war["psM"] = [be]
                    return be

                nq = NQ if LIMIT_QT is None else LIMIT_QT
                pre = next_chains.pop(p, {})
                _qe = {}
                for qq in range(nq):
                    _qe[qq] = pre[("qe", qq)] if ("qe", qq) in pre \
                        else q_chain_A(qq)
                qms = [pre[("qm", qq)] if ("qm", qq) in pre
                       else q_chain_B(qq, _qe[qq]) for qq in range(nq)]
                if pend_cp[0] is not None:
                    cps_box[0] = pend_cp[0]()
                    pend_cp[0] = None
                if p == 0 and pend_start is not None:
                    pend_start()
                    pend_start = None
                be_box = [None]
                for q in range(nq):
                    sl_q = q % 4
                    qe = _qe[q]
                    last_q = (q == nq - 1)
                    ns = next_slot() if last_q else None
                    nkst = {}
                    if ns is not None and h == 0:
                        nkst["pidx"] = ns[0] if ns[0] != "next_half" else 0
                    s_tok = [None] * NJ
                    e_tok = [None] * NJ
                    pv_tok = [None] * NJ

                    def emit_S(j, extra=(), kt_ready=kt_ready, qe=qe, sl_q=sl_q):
                        sl = j % 2
                        deps0 = [kt_ready, qe] + list(extra)
                        if j < 2:
                            deps0 += [war["psS0a"], war["psS0b"]] if sl == 0 else \
                                     [war["psS1a"], war["psS1b"]]
                        P.emit("tensor",
                               lambda e, j=j, sl=sl, sl_q=sl_q:
                                   e.matmul(psS[sl][:, 0:QT],
                                            kt_v[0:64, j * 128:(j + 1) * 128],
                                            qtb[0:64, sl_q * QT:(sl_q + 1) * QT],
                                            start=True, stop=True),
                               deps=deps0)
                        s_tok[j] = P.emit(
                            "tensor",
                            lambda e, j=j, sl=sl, sl_q=sl_q:
                                e.matmul(psS[sl][:, QT:E2],
                                         kt_v[64:128, j * 128:(j + 1) * 128],
                                         qtb[64:128, sl_q * QT:(sl_q + 1) * QT],
                                         start=True, stop=True),
                            inc="pe")
                        s_last[0] = s_tok[j]

                    def emit_E(j, sl_q=sl_q):
                        sl = j % 3
                        deps0 = [s_tok[j], be_box[0]]
                        if j >= 3:
                            deps0.append(pv_tok[j - 3])
                        e_tok[j] = P.emit(
                            "scalar",
                            lambda e, j=j, sl=sl, sl_q=sl_q:
                                e.activation(pt[:, sl * E2:(sl + 1) * E2],
                                             psS[j % 2][:, 0:E2], AF.Exp,
                                             bias=biasb[:, sl_q:sl_q + 1]),
                            deps=deps0, inc="act", nochain=True)

                    def emit_PV(j, va_ready=va_ready):
                        sl = j % 3
                        deps0 = [e_tok[j], va_ready]
                        if j == 0:
                            deps0 += [war["psA0"], war["psA1"]]
                        P.emit("tensor",
                               lambda e, j=j, sl=sl:
                                   e.matmul(psA[0][0:65, :],
                                            va_v[:, j * 130:j * 130 + 65],
                                            pt[:, sl * E2:sl * E2 + QT],
                                            start=(j == 0), stop=(j == NJ - 1)),
                               deps=deps0)
                        pv_tok[j] = P.emit(
                            "tensor",
                            lambda e, j=j, sl=sl:
                                e.matmul(psA[1][0:65, :],
                                         va_v[:, j * 130 + 65:j * 130 + 130],
                                         pt[:, sl * E2 + QT:sl * E2 + E2],
                                         start=(j == 0), stop=(j == NJ - 1)),
                            inc="pe")
                        pv_last[0] = pv_tok[j]

                    emit_S(0)
                    emit_S(1)
                    if q == 0:
                        be_box[0] = q_chain_C_all(qms, nq)
                    for j in range(NJ):
                        emit_E(j)
                        if j + 2 < NJ:
                            emit_S(j + 2, extra=[e_tok[j]])
                        emit_PV(j)
                        if j == 2 and pend_tail[0] is not None:
                            pend_tail[0]()
                            pend_tail[0] = None
                        if (h == 1 and last_q and not serial_kv
                                and p + 1 < npair and j in (5, 6, 9, 10, 13, 14)):
                            qi, isB = {5: (0, False), 6: (0, True),
                                       9: (1, False), 10: (1, True),
                                       13: (2, False), 14: (2, True)}[j]
                            if qi < nq - 1:
                                nc_ = next_chains.setdefault(p + 1, {})
                                if isB:
                                    nc_[("qm", qi)] = q_chain_B(
                                        qi, nc_[("qe", qi)])
                                else:
                                    nc_[("qe", qi)] = q_chain_A(
                                        qi, sx_=(p + 1) % 2)
                        if ns is not None and j % 4 == 3:
                            ki = j // 4
                            if h == 0:
                                k_group(ns[1], ki, nkst)
                                v_group(ns[1], ki, nkst)
                                knorm_one(nkst)
                            else:
                                kv_load_slice(p + 1, ki, nkst,
                                              [s_tok[j], pv_tok[j]])
                    if ns is not None:
                        if h == 0:
                            knorm_batch(nkst)
                            nkst["pidx"] = ns[0] if ns[0] != "next_half" else 0
                        else:
                            nkst["kmax"] = war["kmax_toks"][p + 1]
                            nkst["pidx"] = p + 1
                        kv_state[p + 1 if ns[0] != "next_half" else "next_half"] = nkst
                    qtw[sl_q] = [s_tok[NJ - 1]]
                    biasw[sl_q] = [e_tok[NJ - 1]]
                    war["psS0a"] = war["psS0b"] = [e_tok[NJ - 2]]
                    war["psS1a"] = war["psS1b"] = [e_tok[NJ - 1]]
                    def emit_cps(pvlast=pv_tok[NJ - 1]):
                        c0 = P.emit("vector",
                                    lambda e: e.tensor_copy(astg[0:65, 0:QT],
                                                            psA[0][0:65, :]),
                                    deps=[pvlast, war["astg"]],
                                    inc="dve", nochain=True)
                        c1 = P.emit("vector",
                                    lambda e: e.tensor_copy(astg[0:65, QT:E2],
                                                            psA[1][0:65, :]),
                                    deps=[pvlast], inc="dve", nochain=True)
                        war["psA0"] = [c0]
                        war["psA1"] = [c1]
                        return c0, c1

                    if last_q and not serial_kv and p + 1 < npair:
                        pend_cp[0] = emit_cps
                        cp0 = cp1 = None
                    else:
                        cp0, cp1 = emit_cps()

                    def tail(p=p, q=q, cp0=cp0, cp1=cp1):
                        if cp0 is None:
                            cp0, cp1 = cps_box[0]
                        bm0 = P.emit("tensor",
                                     lambda e: e.matmul(psM[0:64, 0:QT],
                                                        ones_f[64:65, :],
                                                        astg[64:65, 0:QT],
                                                        start=True, stop=True),
                                     deps=[cp0, war["psM"]], inc="pe")
                        rb0 = P.emit("vector",
                                     lambda e: e.tensor_copy(recipb[:, 0:QT],
                                                             psM[0:64, 0:QT]),
                                     deps=[bm0], inc="dve", nochain=True)
                        bm1 = P.emit("tensor",
                                     lambda e: e.matmul(psM[0:64, 0:QT],
                                                        ones_f[64:65, :],
                                                        astg[64:65, QT:E2],
                                                        start=True, stop=True),
                                     deps=[cp1, rb0], inc="pe")
                        rb1 = P.emit("vector",
                                     lambda e: e.tensor_copy(recipb[:, QT:E2],
                                                             psM[0:64, 0:QT]),
                                     deps=[bm1], inc="dve", nochain=True)
                        war["psM"] = [rb1]
                        rcp = P.emit("vector",
                                     lambda e: e.reciprocal(recipb[:, 0:E2],
                                                            recipb[:, 0:E2]),
                                     deps=[rb1], inc="dve", nochain=True)
                        aw0 = P.emit("vector",
                                     lambda e, p=p, q=q:
                                         e.tensor_tensor(
                                             abuf[0:64, p * TH + q * QT:p * TH + (q + 1) * QT],
                                             astg[0:64, 0:QT], recipb[0:64, 0:QT],
                                             ALU.mult),
                                     deps=[rcp, abuf_free_tok], inc="dve", nochain=True)
                        aw1 = P.emit("vector",
                                     lambda e, p=p, q=q:
                                         e.tensor_tensor(
                                             abuf[64:128, p * TH + q * QT:p * TH + (q + 1) * QT],
                                             astg[0:64, QT:E2], recipb[0:64, QT:E2],
                                             ALU.mult),
                                     deps=[rcp, abuf_free_tok], inc="dve", nochain=True)
                        war["astg"] = [aw0, aw1]
                        if p == npair - 1:
                            final_aw[q] = [aw0, aw1]
                        return aw1

                    pend_tail[0] = tail
                war["kt"] = [s_last[0]] + [t for t in war["kt"]
                                           if isinstance(t, tuple) and t[0] == "qx"]
                war["va"] = [pv_last[0]] + [t for t in war["va"]
                                           if isinstance(t, tuple) and t[0] == "qx"]
                if h == 0:
                    war.setdefault("kmax_toks", {})[p] = kmax_tok
            if pend_tail[0] is not None:
                a_last = pend_tail[0]()
                pend_tail[0] = None
            war["kv_next"] = kv_state.pop("next_half", None)
            return final_aw, s_last[0], pv_last[0]

        def wo_ln1_half(h, a_done):
            xn1_tok = [[None] * NCH for _ in range(NQ)]
            last_xt_read = [None]
            wo_last = [None]
            ln_done = {-1: []}
            stats_tok = {}
            BANKS = [(psS[0], 0, "psS0a"), (psS[1], 0, "psS1a"),
                     (psS[0], QT, "psS0b"), (psS[1], QT, "psS1b")]

            def wo_mms(q, mc):
                pss, off, pb = BANKS[mc % 4]
                for kc in range(NCH):
                    dep = [a_done[q], war[pb], tk_wo] if kc == 0 else []
                    P.emit(
                        "tensor",
                        lambda e, kc=kc, mc=mc, q=q, pss=pss, off=off:
                            e.matmul(pss[:, off:off + QT],
                                     wb[1][:, kc * 1024 + mc * 128:
                                           kc * 1024 + (mc + 1) * 128],
                                     abuf[:, kc * TH + q * QT:kc * TH + (q + 1) * QT],
                                     start=(kc == 0), stop=False),
                        deps=dep)
                mm = P.emit("tensor",
                            lambda e, mc=mc, pss=pss, off=off:
                                e.matmul(pss[:, off:off + QT],
                                         bob[0:1, mc * 128:(mc + 1) * 128],
                                         ones_r[0:1, 0:QT], start=False, stop=True),
                            deps=[tk_bo], inc="pe")
                wo_last[0] = mm
                return mm

            def wo_rt(q, mc, mm):
                pss, off, pb = BANKS[mc % 4]
                rt = P.emit(
                    "vector",
                    lambda e, mc=mc, q=q, pss=pss, off=off, h=h:
                        e.tensor_tensor(
                            abuf[:, AHI + mc * QT:AHI + (mc + 1) * QT],
                            pss[:, off:off + QT],
                            xt[:, mc * T + h * TH + q * QT:
                               mc * T + h * TH + (q + 1) * QT],
                            ALU.add),
                    deps=[mm] + ln_done[q - 1], inc="dve")
                last_xt_read[0] = rt
                war[pb] = [rt]
                return rt

            def stats(q, r_tok):
                sx = sx2 = sq_last = None
                for mc in range(NCH):
                    rsl = abuf[:, AHI + mc * QT:AHI + (mc + 1) * QT]
                    dep0 = [r_tok[mc]] + ([war["psA0"]] if mc == 0 else [])
                    sx = P.emit("tensor",
                                lambda e, rsl=rsl, mc=mc:
                                    e.matmul(psA[0][0:1, 0:QT], ones_c[:, :],
                                             rsl, start=(mc == 0),
                                             stop=(mc == NCH - 1)),
                                deps=dep0, inc="pe" if mc == NCH - 1 else None)
                    sqt = P.emit("scalar",
                                 lambda e, rsl=rsl, mc=mc:
                                     e.activation(sqk[:, (mc % 2) * QT:(mc % 2 + 1) * QT],
                                                  rsl, AF.Square),
                                 deps=[r_tok[mc], war["sqk%d" % (mc % 2)]],
                                 inc="act", nochain=True)
                    sq_last = sqt
                    sx2 = P.emit("tensor",
                                 lambda e, mc=mc:
                                     e.matmul(psA[1][0:1, 0:QT], ones_c[:, :],
                                              sqk[:, (mc % 2) * QT:(mc % 2 + 1) * QT],
                                              start=(mc == 0),
                                              stop=(mc == NCH - 1)),
                                 deps=[sqt] + ([war["psA1"]] if mc == 0 else []),
                                 inc="pe")
                    war["sqk%d" % (mc % 2)] = [sx2]
                ab_, bb_ = ln_rows_pre(sx, sx2, psA[0], psA[1])
                war["psA0"] = war["psA1"] = [ab_]
                stats_tok[q] = (ab_, bb_, sx, sq_last)

            def rows_norm(q, wo_mm_last):
                ab_, bb_, sx, sq_last = stats_tok.pop(q)
                be2, ae = ln_rows_post(ab_, bb_)
                n_last = None
                for mc in range(NCH):
                    rsl = abuf[:, AHI + mc * QT:AHI + (mc + 1) * QT]
                    P.emit("vector",
                           lambda e, mc=mc, q=q, rsl=rsl:
                               e.tensor_tensor(abuf[:, mc * TH + q * QT:mc * TH + (q + 1) * QT],
                                               rsl, ab2[:, 0:QT], ALU.mult),
                           deps=[ae, wo_mm_last])
                    xn1_tok[q][mc] = P.emit(
                        "vector",
                        lambda e, mc=mc, q=q:
                            e.tensor_tensor(abuf[:, mc * TH + q * QT:mc * TH + (q + 1) * QT],
                                            abuf[:, mc * TH + q * QT:mc * TH + (q + 1) * QT],
                                            ab2[:, QT:E2], ALU.add),
                        deps=[be2], inc="dve")
                    n_last = xn1_tok[q][mc]
                ln_done[q] = [sx, sq_last, n_last]

            wo_mm_of = {}
            cw_done = None
            for q in range(NQ):
                amm = [wo_mms(q, mc) for mc in range(4)]
                wo_mm_of[q] = wo_last[0]
                if q > 0:
                    rows_norm(q - 1, wo_mm_of[q - 1])
                r_tok = [wo_rt(q, mc, amm[mc]) for mc in range(4)]
                if q > 0:
                    le = router_q_mm(h, q - 1, xn1_tok[q - 1])
                for mc in range(4, NCH):
                    mm = wo_mms(q, mc)
                    wo_mm_of[q] = mm
                    r_tok.append(wo_rt(q, mc, mm))
                    if q > 0 and mc == 5:
                        cwt = router_q_tr(le)
                    if q > 0 and mc == 7:
                        cw_done = router_q_out(h, q - 1, cwt)
                stats(q, r_tok)
            rows_norm(NQ - 1, wo_mm_of[NQ - 1])
            return xn1_tok, last_xt_read[0], wo_last[0], None

        def ln_rows_pre(sx, sx2, ps_sum, ps_sq):
            mt = P.emit("vector",
                        lambda e: e.tensor_scalar(R_m, ps_sum[0:1, 0:QT],
                                                  1.0 / D, None, ALU.mult),
                        deps=[sx, sx2], inc="dve")
            vt = P.emit("vector",
                        lambda e: e.tensor_scalar(R_v, ps_sq[0:1, 0:QT],
                                                  1.0 / D, None, ALU.mult),
                        deps=[sx2], inc="dve")
            if ps_sum is psS[1]:
                war["psS1a"] = [mt]
                war["psS0a"] = [vt]
            P.emit("vector", lambda e: e.tensor_tensor(R_t, R_m, R_m, ALU.mult),
                   deps=[mt])
            tv = P.emit("vector", lambda e: e.tensor_tensor(R_v, R_v, R_t, ALU.subtract),
                        deps=[vt], inc="dve")
            l1 = P.emit("scalar",
                        lambda e: e.activation(R_t, R_v, AF.Ln, bias=SC_eps),
                        deps=[tv], inc="act")
            at = P.emit("scalar",
                        lambda e: e.activation(R_a, R_t, AF.Exp, scale=-0.5),
                        deps=[l1], inc="act")
            ab_ = P.emit("vector", lambda e: e.tensor_copy(rbf[0:1, 0:QT], R_a),
                         deps=[at], inc="dve")
            P.emit("vector", lambda e: e.tensor_tensor(R_t, R_m, R_a, ALU.mult),
                   deps=[at])
            bb_ = P.emit("vector",
                         lambda e: e.tensor_scalar(rbf[0:1, QT:E2], R_t,
                                                   -1.0, None, ALU.mult),
                         inc="dve")
            return ab_, bb_

        def ln_rows_post(ab_, bb_, extra=()):
            am = P.emit("tensor",
                        lambda e: e.matmul(psM[:, 0:QT], ones_r[0:1, 0:128],
                                           rbf[0:1, 0:QT], start=True, stop=True),
                        deps=[ab_, war["psM"]] + list(extra), inc="pe")
            ae = P.emit("vector",
                        lambda e: e.tensor_copy(ab2[:, 0:QT], psM[:, 0:QT]),
                        deps=[am], inc="dve")
            bm2 = P.emit("tensor",
                         lambda e: e.matmul(psM[:, 0:QT], ones_r[0:1, 0:128],
                                            rbf[0:1, QT:E2], start=True, stop=True),
                         deps=[bb_, ae], inc="pe")
            be2 = P.emit("vector",
                         lambda e: e.tensor_copy(ab2[:, QT:E2], psM[:, 0:QT]),
                         deps=[bm2], inc="dve")
            war["psM"] = [be2]
            return be2, ae

        def ln_rows(sx, sx2, ps_sum, ps_sq):
            ab_, bb_ = ln_rows_pre(sx, sx2, ps_sum, ps_sq)
            be2, ae = ln_rows_post(ab_, bb_)
            return ab_, be2, ae

        def router_q_mm(h, q, xn1_q):
            lm = None
            for kc in range(NCH):
                lm = P.emit("tensor",
                            lambda e, kc=kc, q=q:
                                e.matmul(psM[0:16, 0:QT],
                                         wrb[:, kc * 16:(kc + 1) * 16],
                                         abuf[:, kc * TH + q * QT:kc * TH + (q + 1) * QT],
                                         start=(kc == 0), stop=(kc == NCH - 1)),
                            deps=[xn1_q[kc], tk_wr] + ([war["psM"]] if kc == 0 else []),
                            inc="pe" if kc == NCH - 1 else None)
            le = P.emit("vector",
                        lambda e: e.tensor_copy(xf32[0:16, 0:QT], psM[0:16, 0:QT]),
                        deps=[lm, war["xf"]], inc="dve")
            return le

        def router_q_tr(le):
            trt = None
            for st in range(NST):
                trt = P.emit("tensor",
                             lambda e, st=st:
                                 e.transpose(psM[:, st * 16:(st + 1) * 16],
                                             xf32[0:16, st * 128:(st + 1) * 128],
                                             cidfb[0:16, 0:16]),
                             deps=[le, tk_cidf], inc="pe")
            war["xf"] = [trt]
            pv = P.emit("vector",
                        lambda e: e.tensor_copy(prs[:, :], psM[:, 0:NST * 16]),
                        deps=[trt, war["astg"]], inc="dve")
            war["psM"] = [pv]
            p3 = prs.rearrange("p (a b) -> p a b", b=16)
            c3 = cwf.rearrange("p (a b) -> p a b", b=16)
            m4 = biasb[:, 4:4 + NST]
            d4 = biasb[:, 8:8 + NST]
            m4b = m4.rearrange("p (a o) -> p a o", o=1).broadcast_to([128, NST, 16])
            d4b = d4.rearrange("p (a o) -> p a o", o=1).broadcast_to([128, NST, 16])
            P.emit("vector",
                   lambda e: e.tensor_reduce(m4, p3, AX.X, ALU.max),
                   deps=[pv])
            P.emit("vector",
                   lambda e: e.tensor_tensor(c3, p3, m4b, ALU.subtract))
            exq = P.emit("scalar",
                         lambda e: e.activation(prs, cwf, AF.Exp),
                         deps=[(P.cnt["dve"] and ("dve", P.cnt["dve"]))],
                         inc="act")
            P.emit("vector",
                   lambda e: e.tensor_reduce(d4, p3, AX.X, ALU.add),
                   deps=[exq])
            P.emit("vector", lambda e: e.reciprocal(d4, d4))
            P.emit("vector", lambda e: e.tensor_tensor(c3, p3, d4b, ALU.mult))
            P.emit("vector", lambda e: e.tensor_reduce(m4, c3, AX.X, ALU.max))
            P.emit("vector", lambda e: e.tensor_tensor(p3, c3, m4b, ALU.is_ge))
            P.emit("vector", lambda e: e.tensor_scalar(prs, prs, -2.0, None, ALU.mult))
            P.emit("vector", lambda e: e.tensor_tensor(p3, c3, p3, ALU.add))
            P.emit("vector", lambda e: e.tensor_reduce(m4, p3, AX.X, ALU.max))
            P.emit("vector", lambda e: e.tensor_tensor(p3, c3, m4b, ALU.is_ge))
            cwt = P.emit("vector",
                         lambda e: e.tensor_tensor(cwf, cwf, prs, ALU.mult),
                         inc="dve")
            return cwt

        def router_q_out(h, q, cwt):
            cw_tok = [cwt] * NST
            trc = None
            for st in range(NST):
                trc = P.emit("tensor",
                             lambda e, st=st:
                                 e.transpose(psM[0:16, st * 128:(st + 1) * 128],
                                             cwf[:, st * 16:(st + 1) * 16],
                                             cidfb[:, :]),
                             deps=[cw_tok[st], war["psM"]], inc="pe")
            cw_done = P.emit("vector",
                             lambda e, q=q, h=h:
                                 e.tensor_copy(cwT[0:16, h * TH + q * QT:
                                                   h * TH + (q + 1) * QT],
                                               psM[0:16, 0:QT]),
                             deps=[trc], inc="dve")
            war["psM"] = [cw_done]
            war["astg"] = [cw_done]
            return cw_done

        def router_q(h, q, xn1_q):
            le = router_q_mm(h, q, xn1_q)
            cwt = router_q_tr(le)
            return router_q_out(h, q, cwt)

        def ffn_all(xsrcs, cw_dones, src_toks, accw):
            last_w_read = None
            """xsrcs[h](kc, q) -> AP of the xn1 slice for half h;
            src_toks[h][q] -> deps proving that slice is written (h0: the
            reload DMA, h1: the LN1 normalize tokens)."""
            tok_slot = [None, None]
            acc_tok = {}
            cwb_tok = [None] * NQ
            pend_ln = []

            def acc_ap(h, mc, q):
                if h == 1:
                    return xt[:, ACC + mc * TH + q * QT:ACC + mc * TH + (q + 1) * QT]
                return abuf[:, mc * TH + q * QT:mc * TH + (q + 1) * QT]

            acc3v = [abuf[:, 0:NCH * TH].rearrange("p (a t) -> p a t", a=NCH),
                     xt[:, ACC:ACC + NCH * TH].rearrange("p (a t) -> p a t", a=NCH)]
            xsrc3v = [xt[:, 0:NCH * TH].rearrange("p (a t) -> p a t", a=NCH),
                      abuf[:, 0:NCH * TH].rearrange("p (a t) -> p a t", a=NCH)]
            out3 = out_d.rearrange("p (a t) -> p a t", a=NCH)
            ost3 = [outst.rearrange("p (a t) -> p a t", a=2),
                    astg.rearrange("p (a t) -> p a t", a=2)]

            def flush_ln():
                while pend_ln:
                    ln2_q_tail(pend_ln.pop(0))

            pend_stats = []

            def flush_stats():
                while pend_stats:
                    h_, q_ = pend_stats.pop(0)
                    pend_ln.append(ln2_q_stats(h_, q_))

            def ln2_q_stats(h, q):
                r_tok = acc_tok[(h, q)]
                if h == 1:
                    accw[q] = [None] * NCH
                sx = sx2 = None
                for mc in range(NCH):
                    dep0 = [r_tok[mc]] + ([war["psV"]] if mc == 0 else [])
                    sx = P.emit("tensor",
                                lambda e, mc=mc, q=q, h=h:
                                    e.matmul(psV[0:1, 0:QT], ones_c[:, :],
                                             acc_ap(h, mc, q),
                                             start=(mc == 0), stop=(mc == NCH - 1)),
                                deps=dep0, inc="pe" if mc == NCH - 1 else None)
                    sqt = P.emit("scalar",
                                 lambda e, mc=mc, q=q, h=h:
                                     e.activation(sqk[:, (mc % 2) * QT:(mc % 2 + 1) * QT],
                                                  acc_ap(h, mc, q),
                                                  AF.Square),
                                 deps=[r_tok[mc], war["sqk%d" % (mc % 2)]],
                                 inc="act", nochain=True)
                    sx2 = P.emit("tensor",
                                 lambda e, mc=mc:
                                     e.matmul(psM[0:1, 0:QT], ones_c[:, :],
                                              sqk[:, (mc % 2) * QT:(mc % 2 + 1) * QT],
                                              start=(mc == 0),
                                              stop=(mc == NCH - 1)),
                                 deps=[sqt] + ([war["psM"]] if mc == 0 else []),
                                 inc="pe")
                    war["sqk%d" % (mc % 2)] = [sx2]
                ab_, bb_ = ln_rows_pre(sx, sx2, psV, psM)
                war["psV"] = [ab_]
                war["psM"] = [ab_]
                return (h, q, ab_, bb_)

            last_nts = [[]]

            def ln2_q_tail(st):
                h, q, ab_, bb_ = st
                be2, ae = ln_rows_post(ab_, bb_, extra=last_nts[0])
                a2b = ab2[:, 0:QT].rearrange("p (a t) -> p a t",
                                             a=1).broadcast_to([128, 2, QT])
                b2b = ab2[:, QT:E2].rearrange("p (a t) -> p a t",
                                              a=1).broadcast_to([128, 2, QT])
                nts = [None] * NCH
                for mp in range(NCH // 2):
                    sl = mp % 2
                    eng, qsem = ("gpsimd", "qw") if (h == 0 and sl == 0) \
                        else ("sync", "qx")
                    eng2, sem2 = ("gpsimd", "pl") if h == 1 else \
                        ("vector", "dve")
                    ml = P.emit(eng2,
                                lambda e, mp=mp, q=q, h=h, sl=sl, a2b=a2b:
                                    e.tensor_tensor(ost3[sl][:, :, :],
                                                    acc3v[h][:, 2 * mp:2 * mp + 2,
                                                             q * QT:(q + 1) * QT],
                                                    a2b, ALU.mult),
                                deps=[ae, st_outst[sl],
                                      war["astg"] if sl else None], inc=sem2,
                                nochain=True)
                    nt = P.emit(eng2,
                                lambda e, sl=sl, b2b=b2b:
                                    e.tensor_tensor(ost3[sl][:, :, :],
                                                    ost3[sl][:, :, :],
                                                    b2b, ALU.add),
                                deps=[be2, ml], inc=sem2, nochain=True)
                    nts[2 * mp] = nts[2 * mp + 1] = nt
                    last_nts[0] = [nt]
                    ot = P.emit(eng,
                                lambda e, mp=mp, q=q, h=h, sl=sl:
                                    e.dma_start(out3[:, 2 * mp:2 * mp + 2,
                                                     h * TH + q * QT:
                                                     h * TH + (q + 1) * QT],
                                                ost3[sl][:, :, :]),
                                deps=[nt], inc=qsem)
                    st_outst[sl] = ot
                return nts

            def load_moe(si):
                sl = si % 2
                eo = EXP_ORDER[si % 18]
                tg = P.emit("gpsimd",
                            lambda e, sl=sl, eo=eo:
                                e.dma_start(wb[sl][:, 0:4096],
                                            moew[eo, :, 0:4096]),
                            deps=[moe_slot_war[sl]], inc="qw")
                tu = P.emit("gpsimd",
                            lambda e, sl=sl, eo=eo:
                                e.dma_start(wb[sl][:, 4096:8192],
                                            moew[eo, :, 4096:8192]),
                            inc="qw")
                td = P.emit("gpsimd",
                            lambda e, sl=sl, eo=eo:
                                e.dma_start(wb[sl][:, 8192:12288],
                                            moew[eo, :, 8192:12288]),
                            inc="qw")
                tok_slot[sl] = (tg, tu, td)

            load_moe(0)
            load_moe(1)
            def phase_A(ctx, q):
                h = ctx["h"]
                sl = ctx["sl"]
                e_orig = ctx["e_orig"]
                routed = ctx["routed"]
                my_tok = ctx["my_tok"]
                cw_done = cw_dones[h]
                hb = AHI + (q % 2) * (4 * QT)
                if routed:
                    cm = P.emit("tensor",
                                lambda e, eo=e_orig, q=q, h=h:
                                    e.matmul(psM[:, 0:QT],
                                             ceimb[0:16, eo * 128:(eo + 1) * 128],
                                             cwT[0:16, h * TH + q * QT:
                                                 h * TH + (q + 1) * QT],
                                             start=True, stop=True),
                                deps=[cw_done, war["psM"], tk_ceim], inc="pe")
                    cwb_tok[q] = P.emit(
                        "vector",
                        lambda e, q=q: e.tensor_copy(
                            qtb[:, q * QT:(q + 1) * QT], psM[:, 0:QT]),
                        deps=[cm, war["qtw"][q]], inc="dve", nochain=True)
                    war["psM"] = [cwb_tok[q]]
                    war["qtw"][q] = []
                h_tok = [None] * 4
                GB = [(psS[0], 0, "psS0a"), (psS[1], 0, "psS1a"),
                      (psV, 0, "psV")]
                UB = [(psS[0], QT, "psS0b"), (psS[1], QT, "psS1b")]
                for fc in range(4):
                    gp, go, ga = GB[fc % 3]
                    up, uo, gb = UB[fc % 2]
                    gm = None
                    for kc in range(NCH):
                        dep = ([my_tok[0]] + src_toks[h][q] if fc == 0 else []) \
                            + [war[ga]] if kc == 0 else []
                        gm = P.emit(
                            "tensor",
                            lambda e, sl=sl, kc=kc, fc=fc, q=q, gp=gp, go=go:
                                e.matmul(gp[:, go:go + QT],
                                         wb[sl][:, kc * 512 + fc * 128:
                                               kc * 512 + (fc + 1) * 128],
                                         xsrcs[h](kc, q),
                                         start=(kc == 0), stop=(kc == NCH - 1)),
                            deps=dep, inc="pe" if kc == NCH - 1 else None)
                    um = None
                    for kc in range(NCH):
                        dep = ([my_tok[1]] if fc == 0 else []) + [war[gb]] \
                            if kc == 0 else []
                        um = P.emit(
                            "tensor",
                            lambda e, sl=sl, kc=kc, fc=fc, q=q, up=up, uo=uo:
                                e.matmul(up[:, uo:uo + QT],
                                         wb[sl][:, 4096 + kc * 512 + fc * 128:
                                               4096 + kc * 512 + (fc + 1) * 128],
                                         xsrcs[h](kc, q),
                                         start=(kc == 0), stop=(kc == NCH - 1)),
                            deps=dep, inc="pe" if kc == NCH - 1 else None)
                    sl_t = P.emit("scalar",
                                  lambda e, gp=gp, go=go:
                                      e.activation(sq[:, :], gp[:, go:go + QT],
                                                   AF.Silu),
                                  deps=[gm, war["sq"]], inc="act", nochain=True)
                    war[ga] = [sl_t]
                    hm = P.emit("vector",
                                lambda e, fc=fc, up=up, uo=uo, hb=hb:
                                    e.tensor_tensor(
                                        abuf[:, hb + fc * QT:hb + (fc + 1) * QT],
                                        up[:, uo:uo + QT], sq[:, :], ALU.mult),
                                deps=[um, sl_t, war["H%d" % (q % 2)]],
                                inc="dve", nochain=True)
                    war[gb] = [hm]
                    war["sq"] = [hm]
                    if routed:
                        hm = P.emit(
                            "gpsimd",
                            lambda e, fc=fc, q=q, hb=hb:
                                e.tensor_tensor(
                                    abuf[:, hb + fc * QT:hb + (fc + 1) * QT],
                                    abuf[:, hb + fc * QT:hb + (fc + 1) * QT],
                                    qtb[:, q * QT:(q + 1) * QT],
                                    ALU.mult),
                            deps=[cwb_tok[q], hm], inc="pl")
                    h_tok[fc] = hm
                if routed:
                    war["qtw"][q] = [h_tok[3]]
                return h_tok

            def phase_B(ctx, q, h_tok):
                nonlocal last_w_read
                flush_stats()
                ei = ctx["ei"]
                sl = ctx["sl"]
                h = ctx["h"]
                hb = AHI + (q % 2) * (4 * QT)
                dm = None
                for mc in range(NCH):
                    pa = psA[mc % 2]
                    pan = "psA%d" % (mc % 2)
                    for fc in range(4):
                        dep = [h_tok[fc]] + ([war[pan]] if fc == 0 else []) \
                            + ([ctx["my_tok"][2]] if mc == 0 and fc == 0 else [])
                        dm = P.emit(
                            "tensor",
                            lambda e, sl=sl, fc=fc, mc=mc, pa=pa, hb=hb:
                                e.matmul(pa[:, 0:QT],
                                         wb[sl][:, 8192 + fc * 1024 + mc * 128:
                                               8192 + fc * 1024 + (mc + 1) * 128],
                                         abuf[:, hb + fc * QT:hb + (fc + 1) * QT],
                                         start=(fc == 0),
                                         stop=(fc == 3 and ei != 17)),
                            deps=dep, inc="pe" if (fc == 3 and ei != 17) else None)
                    if ei == 17:
                        dm = P.emit(
                            "tensor",
                            lambda e, mc=mc, q=q, h=h, pa=pa:
                                e.matmul(pa[:, 0:QT], cidb[:, :],
                                         xsrcs[h](mc, q),
                                         start=False, stop=True),
                            deps=[tk_cid], inc="pe")
                    last_w_read = dm
                    if ei == 0:
                        ac = P.emit(
                            "scalar",
                            lambda e, mc=mc, q=q, pa=pa, h=h:
                                e.activation(acc_ap(h, mc, q), pa[:, 0:QT],
                                             AF.Copy),
                            deps=[dm, accw[q][mc]], inc="act", nochain=True)
                        acc_tok.setdefault((h, q), [None] * NCH)
                    else:
                        ac = P.emit(
                            "vector",
                            lambda e, mc=mc, q=q, pa=pa, h=h:
                                e.tensor_tensor(
                                    acc_ap(h, mc, q),
                                    pa[:, 0:QT],
                                    acc_ap(h, mc, q), ALU.add),
                            deps=[dm, acc_tok[(h, q)][mc]], inc="dve")
                    acc_tok[(h, q)][mc] = ac
                    war[pan] = [ac]
                war["H%d" % (q % 2)] = [dm]
                flush_ln()
                if ei == 17:
                    pend_stats.append((h, q))

            seq = [(si, q) for si in range(36) for q in range(NQ)]

            def make_ctx(si):
                return {"si": si, "h": 1 - si // 18, "ei": si % 18, "sl": si % 2,
                        "e_orig": EXP_ORDER[si % 18],
                        "routed": EXP_ORDER[si % 18] < 16,
                        "my_tok": tok_slot[si % 2]}

            ctxs = {0: make_ctx(0)}
            ht = {}
            for i, (si, q) in enumerate(seq):
                if i == 0:
                    ht[(si, q)] = phase_A(ctxs[0], q)
                    if callable(cw_dones[1]):
                        cw_dones[1] = cw_dones[1]()
                if i + 1 < len(seq):
                    nsi, nq2 = seq[i + 1]
                    if nsi not in ctxs:
                        ctxs[nsi] = make_ctx(nsi)
                    ht[seq[i + 1]] = phase_A(ctxs[nsi], nq2)
                phase_B(ctxs[si], q, ht.pop((si, q)))
                if q == NQ - 1:
                    moe_slot_war[si % 2] = [last_w_read]
                    if si + 2 < 36:
                        load_moe(si + 2)
                    ctxs.pop(si, None)
            flush_stats()
            flush_ln()

        def stop_here():
            allc = [("pe", P.cnt["pe"]), ("act", P.cnt["act"]),
                    ("dve", P.cnt["dve"]), ("qw", P.cnt["qw"])]
            P.emit("gpsimd",
                   lambda e: e.dma_start(out_d[0:128, 0:NCH * TH],
                                         abuf[:, 0:NCH * TH]),
                   deps=allc, inc="qw")
            P.emit("gpsimd",
                   lambda e: e.dma_start(out_d[0:128, NCH * TH:2 * NCH * TH],
                                         xt[:, 0:NCH * TH]),
                   deps=allc, inc="qw")
            P.wait_only("sync", [("qx", P.cnt["qx"]), ("qw", P.cnt["qw"]),
                                 ("qs", P.cnt["qs"])])

        if STAGE >= 1:
            a0, s_mm0, pv_mm0 = attention_half(0, None, next_qkv_slot=0)
            tk_wo = P.emit("gpsimd",
                           lambda e: e.dma_start(wb[1][:, 0:8192], woi),
                           inc="qw")
        if STAGE >= 2:
            xn1_h0, xt_read0, wo0, _ = wo_ln1_half(0, a0)
        if STAGE >= 3:
            sp = P.emit("scalar",
                        lambda e: e.dma_start(scr, abuf[:, 0:NCH * TH]),
                        deps=[[t for row in xn1_h0 for t in row]], inc="qs",
                        nochain=True)
            abuf_free = [sp]
            cw_box = [None]

            def pend0():
                le0 = router_q_mm(0, NQ - 1, xn1_h0[NQ - 1])
                cwt0 = router_q_tr(le0)
                cw_box[0] = router_q_out(0, NQ - 1, cwt0)
                abuf_free.append(le0)
        if STAGE >= 4:
            kvn = war.get("kv_next")
            a1, s_mm1, pv_mm1 = attention_half(1, abuf_free,
                                               pend_start=pend0)
        if STAGE >= 5:
            xn1_h1, xt_read1, wo1, _ = wo_ln1_half(1, a1)
            cw0 = cw_box[0]
            cw1 = lambda: router_q(1, NQ - 1, xn1_h1[NQ - 1])
            rl = P.emit("scalar",
                        lambda e: e.dma_start(xt[:, 0:NCH * TH], scr),
                        deps=[xt_read1, sp], inc="qs", nochain=True)
            moe_slot_war[0] = [pv_mm1, qkv_war[0], qkv_war[1]]
            moe_slot_war[1] = [s_mm1, wo1]
        if STAGE >= 6:
            accw0 = [[xt_read1] * NCH for _ in range(NQ)]
            ffn_all(
                [lambda kc, q: xt[:, kc * TH + q * QT:kc * TH + (q + 1) * QT],
                 lambda kc, q: abuf[:, kc * TH + q * QT:kc * TH + (q + 1) * QT]],
                [cw0, cw1],
                [[[rl]] * NQ, [list(xn1_h1[q]) for q in range(NQ)]],
                accw0)
            P.wait_only("sync", [("qx", P.cnt["qx"]), ("qw", P.cnt["qw"]),
                                 ("qs", P.cnt["qs"])])
        else:
            stop_here()

        sems = {s: ctx.enter_context(nc.semaphore(s)) for s in SEMS}
        block = ctx.enter_context(nc.Block())

        def player(eng):
            def run(e):
                with nc.allow_low_precision(reason="bf16 stream by design"):
                    for waits, fn, inc, n in P.st[eng]:
                        for s, v in waits:
                            e.wait_ge(sems[s], v)
                        if fn is None:
                            continue
                        inst = fn(e)
                        if inc is not None:
                            inst.then_inc(sems[inc], n)
            return run

        block.tensor(player("tensor"))
        block.scalar(player("scalar"))
        block.vector(player("vector"))
        block.gpsimd(player("gpsimd"))
        block.sync(player("sync"))

    return nc, {e: len(P.st[e]) for e in ENGS}


def _img_chunks(w):
    K, M = w.shape
    return np.ascontiguousarray(
        w.reshape(K // 128, 128, M).transpose(1, 0, 2).reshape(128, (K // 128) * M))


def prep_inputs(L0_attn_x, L0_attn_Wq, L0_attn_Wk, L0_attn_Wv, L0_attn_Wo,
                L0_attn_bo, L0_ffn_Wr, L0_ffn_Wg, L0_ffn_Wu, L0_ffn_Wd,
                L0_ffn_Sg, L0_ffn_Su, L0_ffn_Sd):
    f = np.float32
    x = np.asarray(L0_attn_x, f)
    Wq = np.asarray(L0_attn_Wq, f); Wk = np.asarray(L0_attn_Wk, f)
    Wv = np.asarray(L0_attn_Wv, f); Wo = np.asarray(L0_attn_Wo, f)
    bo = np.asarray(L0_attn_bo, f)
    Wr = np.asarray(L0_ffn_Wr, f)
    Wg = np.asarray(L0_ffn_Wg, f); Wu = np.asarray(L0_ffn_Wu, f)
    Wd = np.asarray(L0_ffn_Wd, f)
    Sg = np.asarray(L0_ffn_Sg, f); Su = np.asarray(L0_ffn_Su, f)
    Sd = np.asarray(L0_ffn_Sd, f)

    def pair_img(W):
        return W.reshape(8, 128, 8, 128).transpose(2, 1, 0, 3).reshape(8, 128, 1024)

    attnw = np.ascontiguousarray(
        np.concatenate([pair_img(Wq), pair_img(Wk), pair_img(Wv)], axis=2))
    woi = _img_chunks(Wo)
    boi = np.ascontiguousarray(bo.reshape(1, 1024))
    wri = _img_chunks(Wr)
    moe = []
    for e in range(16):
        moe.append(np.concatenate(
            [_img_chunks(Wg[e]), _img_chunks(Wu[e]), _img_chunks(Wd[e])], axis=1))
    for half in range(2):
        sl = slice(512 * half, 512 * (half + 1))
        moe.append(np.concatenate(
            [_img_chunks(Sg[:, sl]), _img_chunks(Su[:, sl]),
             _img_chunks(np.ascontiguousarray(Sd[sl, :]))], axis=1))
    moew = np.ascontiguousarray(np.stack(moe))

    cid = np.eye(128, dtype=bf16)
    cidf = np.eye(128, dtype=f)
    ceim = np.zeros((16, 2048), dtype=bf16)
    for e in range(16):
        ceim[e, e * 128:(e + 1) * 128] = bf16(1.0)

    shared = dict(attnw=attnw, woi=woi, boi=boi, wri=wri, moew=moew,
                  cid=cid, cidf=cidf, ceim=ceim)
    in_maps = []
    for c in range(x.shape[0]):
        xi = np.ascontiguousarray(
            x[c].reshape(T, 8, 128).transpose(2, 1, 0).reshape(128, 8 * T))
        m = dict(shared)
        m["xti"] = xi
        in_maps.append(m)
    return in_maps


def kernel(**inputs):
    in_maps = prep_inputs(**inputs)
    nc, _counts = build_program()
    res = run_bass_kernel_spmd(nc, in_maps, core_ids=list(range(8)))
    outs = []
    for r in res.results:
        img = np.asarray(r["out"])
        y = img.reshape(128, 8, T).transpose(2, 1, 0).reshape(T, 1024)
        outs.append(y)
    return np.stack(outs).astype(np.float32)


if __name__ == "__main__":
    nc, counts = build_program()
    print("instruction counts:", counts)
